# version 20
# baseline (speedup 1.0000x reference)
import contextlib
import math
import numpy as np
import ml_dtypes

import concourse.bass as bass
import concourse.mybir as mybir
from concourse.bass_utils import run_bass_kernel_spmd

F32 = mybir.dt.float32
BF16 = mybir.dt.bfloat16
AF = mybir.ActivationFunctionType
ALU = mybir.AluOpType
AX = mybir.AxisListType
BFD = ml_dtypes.bfloat16

D_MODEL = 2048
NDC = 16
TOK = 1024
SEQ = 4096
NH = 16
C_ATT = 1.0 / math.sqrt(128.0)
SHIFT = 12.0
NEXP = 8
INTER = 8192
NCTRL = 130156
NSAMP = INTER * D_MODEL
RSTEP = (NCTRL - 1.0) / (NSAMP - 1.0)
NCOL = NSAMP // 128
CPC = NCOL // 8
LN_EPS = 1e-5
TERN_EPS = 1e-8
GROUPS = [[0, 1, 2, 3], [4, 5, 6, 7]]

ENG_NAMES = ("sync", "tensor", "vector", "scalar", "gpsimd")


class Sched:

    NSEM_DMA = 12

    def __init__(self, capture=False, dep_graph=None):
        self.ops = []
        self.cum = {e: 0 for e in ENG_NAMES}
        self.ndma = {e: 0 for e in ENG_NAMES}
        self.ncc = 0
        self.capture = capture
        self.dep_graph = dep_graph
        self.footprints = []

    def add(self, eng, fn, deps=None, inc=None, dma=False):
        assert eng in ENG_NAMES
        idx = len(self.ops)
        kind = "op"
        if inc == 1 and eng == "gpsimd" and not dma:
            pass
        if dma or eng == "sync":
            kind = "dma"
        if inc == 1 and not dma and eng == "gpsimd" and fn.__code__.co_names and "collective_compute" in fn.__code__.co_names:
            kind = "cc"
        if kind == "dma":
            n = self.ndma[eng]
            self.ndma[eng] += 1
            self.ops.append(dict(eng=eng, fn=fn, mdeps=list(deps) if deps else [],
                                 kind="dma", n=n))
        elif kind == "cc":
            self.ncc += 1
            self.ops.append(dict(eng=eng, fn=fn, mdeps=list(deps) if deps else [],
                                 kind="cc", n=self.ncc))
        else:
            self.cum[eng] += 1
            self.ops.append(dict(eng=eng, fn=fn, mdeps=list(deps) if deps else [],
                                 kind="op", n=self.cum[eng]))
        return idx

    def dma(self, eng, fn, deps=None):
        return self.add(eng, fn, deps=deps, dma=True)

    SBUF_NAMES = frozenset(("AB", "Cc", "Dd", "Ee", "idb", "idf", "onb", "onf", "orf",
                            "msk", "biasq", "gtb", "ebc", "b4s", "b2s", "rr4", "rr2",
                            "shc", "slb"))
    PSUM_NAMES = frozenset(("pAB", "pC", "pD", "pF", "pR1", "pR2", "pT"))

    @classmethod
    def _footprint(cls, mi):
        reads, writes = [], []
        for lst, acc in ((mi.ins, reads), (mi.outs, writes)):
            for a in lst:
                try:
                    bap = a.bass_ap
                    if bap is None:
                        continue
                    name = bap.tensor.name
                    shape = list(bap.tensor.shape)
                    off = bap.offset
                    ap = list(bap.ap)
                    if name in cls.PSUM_NAMES:
                        acc.append((name, 0, 1 << 40))
                    elif name in cls.SBUF_NAMES:
                        rowlen = shape[-1]
                        col0 = off % rowlen
                        span = 0
                        for (step, count) in ap:
                            if step == rowlen and count <= 128:
                                continue
                            span += (count - 1) * step
                        acc.append((name, col0, col0 + span + 1))
                    else:
                        span = 0
                        for (step, count) in ap:
                            span += (count - 1) * step
                        acc.append((name, off, off + span + 1))
                except Exception:
                    acc.append(("__unknown__", 0, 1 << 40))
        return reads, writes

    def emit(self, nc, stack):
        csem = {e: stack.enter_context(nc.semaphore(f"sq_{e}")) for e in ENG_NAMES}
        dsem = {e: [stack.enter_context(nc.semaphore(f"sd_{e}{j}")) for j in range(self.NSEM_DMA)]
                for e in ("sync", "gpsimd", "scalar", "vector")}
        ccsem = stack.enter_context(nc.semaphore("sq_cc"))
        block = stack.enter_context(nc.Block())
        ops = self.ops
        cap = self.capture
        dg = self.dep_graph
        fps = self.footprints
        K = self.NSEM_DMA

        def need_wait(d, i):
            return not (ops[d]["eng"] == "tensor" and ops[i]["eng"] == "tensor"
                        and ops[d]["kind"] == "op" and ops[i]["kind"] == "op")

        tinc = [False] * len(ops)
        tval = [0] * len(ops)
        if not cap and dg is not None:
            for i, o in enumerate(ops):
                for d in dg[i] + o["mdeps"]:
                    if d >= 0 and need_wait(d, i) and ops[d]["eng"] == "tensor" and ops[d]["kind"] == "op":
                        tinc[d] = True
            run = 0
            for i, o in enumerate(ops):
                if o["eng"] == "tensor" and o["kind"] == "op":
                    if tinc[i]:
                        run += 1
                    tval[i] = run

        def wait_for(eng, water, d):
            o = ops[d]
            if o["kind"] == "dma":
                key = (o["eng"], o["n"] % K)
                val = 16 * (o["n"] // K + 1)
                sem = dsem[o["eng"]][o["n"] % K]
            elif o["kind"] == "cc":
                key = ("cc", 0)
                val = o["n"]
                sem = ccsem
            else:
                key = (o["eng"], -1)
                val = tval[d] if o["eng"] == "tensor" else o["n"]
                sem = csem[o["eng"]]
            if water.get(key, 0) < val:
                eng.wait_ge(sem, val)
                water[key] = val

        def runner(ename):
            def body(eng):
                water = {}
                for i, o in enumerate(ops):
                    if o["eng"] != ename:
                        continue
                    if not cap:
                        for d in (dg[i] if dg is not None else ([i - 1] if i > 0 else [])):
                            if d >= 0 and need_wait(d, i):
                                wait_for(eng, water, d)
                        for d in o["mdeps"]:
                            if d >= 0 and need_wait(d, i):
                                wait_for(eng, water, d)
                        if o["kind"] == "dma":
                            j = o["n"] % K
                            prevval = 16 * (o["n"] // K)
                            if prevval > 0 and water.get((ename, j), 0) < prevval:
                                eng.wait_ge(dsem[ename][j], prevval)
                                water[(ename, j)] = prevval
                    inst = o["fn"](eng)
                    if o["kind"] == "dma":
                        inst.then_inc(dsem[ename][o["n"] % K], 16)
                    elif o["kind"] == "cc":
                        inst.then_inc(ccsem, 1)
                    elif ename == "tensor" and not cap:
                        if tinc[i]:
                            inst.then_inc(csem[ename], 1)
                    else:
                        inst.then_inc(csem[ename], 1)
                    if cap:
                        while len(fps) <= i:
                            fps.append(None)
                        fps[i] = self._footprint(inst.ins)
            return body

        block.sync(runner("sync"))
        block.tensor(runner("tensor"))
        block.vector(runner("vector"))
        block.scalar(runner("scalar"))
        block.gpsimd(runner("gpsimd"))


def compute_dep_graph(ops, fps):
    writes = {}
    reads = {}
    deps = [[] for _ in ops]
    out_writers = []
    for i in range(len(ops)):
        fp = fps[i] if i < len(fps) and fps[i] is not None else ([("__unknown__", 0, 1 << 40)], [("__unknown__", 0, 1 << 40)])
        rd, wr = fp
        dd = set()
        for (name, lo, hi) in rd:
            for rec in writes.get(name, ()):
                if rec[0] < hi and lo < rec[1]:
                    dd.add(rec[2])
            reads.setdefault(name, []).append([lo, hi, i])
        for (name, lo, hi) in wr:
            wl = writes.setdefault(name, [])
            rl = reads.setdefault(name, [])
            keepw = []
            for rec in wl:
                if rec[0] < hi and lo < rec[1]:
                    dd.add(rec[2])
                    if not (lo <= rec[0] and rec[1] <= hi):
                        keepw.append(rec)
                else:
                    keepw.append(rec)
            keepw.append([lo, hi, i])
            writes[name] = keepw
            keepr = []
            for rec in rl:
                if rec[0] < hi and lo < rec[1]:
                    dd.add(rec[2])
                    if not (lo <= rec[0] and rec[1] <= hi):
                        keepr.append(rec)
                else:
                    keepr.append(rec)
            reads[name] = keepr
            if name in ("out", "dbg"):
                out_writers.append(i)
        dd.discard(i)
        deps[i] = sorted(dd)
    if out_writers:
        deps[-1] = sorted(set(deps[-1]) | set(out_writers))
    return deps


def host_constants():
    c = {}
    c["ident_bf"] = np.eye(128, dtype=BFD)
    c["ident_f"] = np.eye(128, dtype=np.float32)
    c["ones_bf"] = np.ones((128, 1), dtype=BFD)
    c["ones_f"] = np.ones((128, 1), dtype=np.float32)
    c["ones_row_f"] = np.ones((1, 128), dtype=np.float32)
    masks = np.zeros((4, 128, 512), dtype=np.float32)
    for j in range(4):
        for p in range(128):
            masks[j, p, :j * 128 + p] = -1e30
    c["masks"] = np.ascontiguousarray(masks.transpose(1, 0, 2).reshape(128, 2048))
    half = 64
    inv = 1.0 / (10000.0 ** (np.arange(half, dtype=np.float32) / half))
    fr = np.outer(np.arange(SEQ, dtype=np.float32), inv)
    c["cos_f"] = np.ascontiguousarray(np.cos(fr).T).astype(np.float32)
    c["sin_f"] = np.ascontiguousarray(np.sin(fr).T).astype(np.float32)
    pR = (np.arange(128, dtype=np.float64) * RSTEP).astype(np.float32)
    c["spl_b4"] = np.stack([np.ones(128, np.float32), pR, pR ** 2, pR ** 3], axis=0).astype(np.float32)
    c["spl_b2"] = np.stack([np.ones(128, np.float32), pR], axis=0).astype(np.float32)
    selb = np.zeros((8, 8 * 128), dtype=np.float32)
    for e in range(8):
        selb[e, e * 128:(e + 1) * 128] = 1.0
    c["selb"] = selb
    return c


def spline_cols(kind):
    if kind == "gu":
        og = np.arange(64, dtype=np.int64)[:, None, None]
        dc = np.arange(16, dtype=np.int64)[None, :, None]
        j = np.arange(128, dtype=np.int64)[None, None, :]
        s0 = ((og * 128 + j) * D_MODEL + dc * 128).reshape(-1)
    else:
        dc2 = np.arange(16, dtype=np.int64)[:, None, None]
        hc = np.arange(64, dtype=np.int64)[None, :, None]
        j = np.arange(128, dtype=np.int64)[None, None, :]
        s0 = ((dc2 * 128 + j) * INTER + hc * 128).reshape(-1)
    return s0


def spline_host_rows(cp, s0):
    t = s0.astype(np.float64) * RSTEP
    I = np.clip(np.floor(t).astype(np.int64), 0, NCTRL - 2)
    phi = (t - I).astype(np.float32)
    Q = np.stack([cp[np.clip(I - 1 + k, 0, NCTRL - 1)] for k in range(5)], axis=0)
    return Q.astype(np.float32), np.stack([phi, phi * phi, phi * phi * phi], axis=0).astype(np.float32)


def build_program(debug_tap=None, _sched=None):
    nc = bass.Bass("TRN2", target_bir_lowering=False, debug=False)

    def din(name, shape, dt=F32):
        return nc.dram_tensor(name, list(shape), dt, kind="ExternalInput").ap()

    def dint(name, shape, dt=F32):
        return nc.dram_tensor(name, list(shape), dt, kind="Internal")

    x0_in = din("x0", [NDC, 128, TOK])
    l0c = {w: din(f"w{w}_codes", [D_MODEL, D_MODEL]) for w in "qkvo"}
    l0s = {w: din(f"w{w}_scales", [D_MODEL, D_MODEL]) for w in "qkvo"}
    bias_qkvo = din("bias_qkvo", [128, 4 * NDC])
    gateT_in = din("gateT", [128, NDC * NEXP])
    ebias_in = din("ebias", [NEXP])
    w1R = din("w1R", [NEXP, 8, 128, 2048])
    w3R = din("w3R", [NEXP, 8, 128, 2048])
    w2R = din("w2R", [NEXP, 16, 128, 1024])
    sw1R = din("sw1R", [8, 128, 2048])
    sw3R = din("sw3R", [8, 128, 2048])
    sw2R = din("sw2R", [16, 128, 1024])
    l1q_nat = din("l1q_nat", [512, D_MODEL])
    l1k_nat = din("l1k_nat", [512, D_MODEL])
    l1v_nat = din("l1v_nat", [512, D_MODEL])
    l1o_nat = din("l1o_nat", [D_MODEL, 512])
    spl = {}
    for w in ("g", "u", "d"):
        spl[w + "_Q"] = din(f"spl_{w}_Q", [5, CPC])
        spl[w + "_phi"] = din(f"spl_{w}_phi", [3, CPC])
    ident_bf = din("ident_bf", [128, 128], BF16)
    ident_f = din("ident_f", [128, 128])
    ones_bf = din("ones_bf", [128, 1], BF16)
    ones_f = din("ones_f", [128, 1])
    ones_row_f = din("ones_row_f", [1, 128])
    masks_in = din("masks", [128, 2048])
    cos_in = din("cos_f", [64, SEQ])
    sin_in = din("sin_f", [64, SEQ])
    b4_in = din("spl_b4", [4, 128])
    b2_in = din("spl_b2", [2, 128])
    selb_in = din("selb", [8, 1024])

    out_final = nc.dram_tensor("out", [NDC, 128, TOK], F32, kind="ExternalOutput").ap()
    dbg = nc.dram_tensor("dbg", [NDC, 128, TOK], F32, kind="ExternalOutput").ap() if debug_tap else None

    x_dram = dint("x_dram", [NDC, 128, TOK])
    sub_dram = dint("sub_dram", [NDC, 128, TOK], BF16)
    k_ag_in = dint("k_ag_in", [D_MODEL, TOK], BF16)
    v_ag_in = dint("v_ag_in", [D_MODEL, TOK], BF16)
    k_ag_out = dint("k_ag_out", [4 * D_MODEL, TOK], BF16)
    v_ag_out = dint("v_ag_out", [4 * D_MODEL, TOK], BF16)
    xh_ag_in = dint("xh_ag_in", [D_MODEL, TOK], BF16)
    xh_ag_out = dint("xh_ag_out", [4 * D_MODEL, TOK], BF16)
    rs_in = dint("rs_in", [4 * D_MODEL, TOK], BF16)
    rs_out = dint("rs_out", [D_MODEL, TOK], BF16)
    g_sl_in = dint("g_sl_in", [8 * 128, D_MODEL], BF16)
    u_sl_in = dint("u_sl_in", [8 * 128, D_MODEL], BF16)
    d_sl_in = dint("d_sl_in", [2 * 128, INTER], BF16)
    g_sl = dint("g_sl", [64 * 128, D_MODEL], BF16)
    u_sl = dint("u_sl", [64 * 128, D_MODEL], BF16)
    d_sl = dint("d_sl", [16 * 128, INTER], BF16)
    w1B = dint("w1B", [NEXP, 8, 128, 2048], BF16)
    w3B = dint("w3B", [NEXP, 8, 128, 2048], BF16)
    w2B = dint("w2B", [NEXP, 16, 128, 1024], BF16)
    sw1B = dint("sw1B", [8, 128, 2048], BF16)
    sw3B = dint("sw3B", [8, 128, 2048], BF16)
    sw2B = dint("sw2B", [16, 128, 1024], BF16)
    rows_dram = dint("rows_dram", [8, CPC])
    ffn_acc = dint("ffn_acc", [NDC, 128, TOK])
    l1w_sc = dint("l1w_sc", [4, 128, 8192], BF16)

    st = contextlib.ExitStack()
    with st:
        AB = st.enter_context(nc.sbuf_tensor("AB", [128, 65536], BF16))
        C = st.enter_context(nc.sbuf_tensor("Cc", [128, 6144], F32))
        D = st.enter_context(nc.sbuf_tensor("Dd", [128, 12288], BF16))
        E = st.enter_context(nc.sbuf_tensor("Ee", [128, 2048], F32))
        idb = st.enter_context(nc.sbuf_tensor("idb", [128, 128], BF16))
        idf = st.enter_context(nc.sbuf_tensor("idf", [128, 128], F32))
        onb = st.enter_context(nc.sbuf_tensor("onb", [128, 1], BF16))
        onf = st.enter_context(nc.sbuf_tensor("onf", [128, 1], F32))
        orf = st.enter_context(nc.sbuf_tensor("orf", [1, 128], F32))
        msk = st.enter_context(nc.sbuf_tensor("msk", [128, 2048], BF16))
        biasq = st.enter_context(nc.sbuf_tensor("biasq", [128, 4 * NDC], F32))
        gtb = st.enter_context(nc.sbuf_tensor("gtb", [128, NDC * NEXP], BF16))
        ebc = st.enter_context(nc.sbuf_tensor("ebc", [NEXP, 1], F32))
        b4s = st.enter_context(nc.sbuf_tensor("b4s", [4, 128], F32))
        b2s = st.enter_context(nc.sbuf_tensor("b2s", [2, 128], F32))
        rr4 = st.enter_context(nc.sbuf_tensor("rr4", [4, 1024], F32))
        rr2 = st.enter_context(nc.sbuf_tensor("rr2", [2, 2048], F32))
        shc = st.enter_context(nc.sbuf_tensor("shc", [128, 1], F32))
        slb = st.enter_context(nc.sbuf_tensor("slb", [8, 1024], F32))
        pAB = st.enter_context(nc.psum_tensor("pAB", [128, 1024], F32))
        pA = pAB[:, 0:512]
        pB = pAB[:, 512:1024]
        pC = st.enter_context(nc.psum_tensor("pC", [128, 512], F32))
        pD = st.enter_context(nc.psum_tensor("pD", [128, 512], F32))
        pF = st.enter_context(nc.psum_tensor("pF", [128, 512], F32))
        pR1 = st.enter_context(nc.psum_tensor("pR1", [128, 512], F32))
        pR2 = st.enter_context(nc.psum_tensor("pR2", [128, 512], F32))
        pT = st.enter_context(nc.psum_tensor("pT", [128, 1024], BF16))

        s = _sched if _sched is not None else Sched(capture=True)

        MEAN = E[0:1, 0:1024]
        RSTD = E[0:1, 1024:2048]
        T1 = rr2[0:1, 1024:2048]
        T2 = C[0:1, 5120:6144]

        s.dma("sync", lambda e: e.dma_start(out=idb[:, :], in_=ident_bf[:, :]))
        s.dma("sync", lambda e: e.dma_start(out=idf[:, :], in_=ident_f[:, :]))
        s.dma("sync", lambda e: e.dma_start(out=onb[:, :], in_=ones_bf[:, :]))
        s.dma("sync", lambda e: e.dma_start(out=onf[:, :], in_=ones_f[:, :]))
        s.dma("sync", lambda e: e.dma_start(out=orf[:, :], in_=ones_row_f[:, :]))
        s.dma("gpsimd", lambda e: e.dma_start(out=msk[:, :], in_=masks_in[:, :]))
        s.dma("sync", lambda e: e.dma_start(out=biasq[:, :], in_=bias_qkvo[:, :]))
        s.dma("gpsimd", lambda e: e.dma_start(out=gtb[:, :], in_=gateT_in[:, :]))
        s.dma("sync", lambda e: e.dma_start(out=ebc[:, 0], in_=ebias_in[:]))
        s.dma("sync", lambda e: e.dma_start(out=b4s[:, :], in_=b4_in[:, :]))
        s.dma("sync", lambda e: e.dma_start(out=b2s[:, :], in_=b2_in[:, :]))
        s.dma("sync", lambda e: e.dma_start(out=x_dram.ap(), in_=x0_in[:, :, :]))
        s.add("vector", lambda e: e.memset(shc[:, :], -SHIFT))
        s.dma("sync", lambda e: e.dma_start(out=slb[:, :], in_=selb_in[:, :]))

        def stats_mm(chunk_f32, sq_f32, first, last):
            for th in range(2):
                sl = slice(th * 512, (th + 1) * 512)
                rS = pR1[0:1, :] if th == 0 else pR1[32:33, :]
                rQ = pR2[0:1, :] if th == 0 else pR2[32:33, :]
                s.add("tensor", lambda e, sl=sl, rS=rS, ch=chunk_f32, first=first, last=last: e.matmul(
                    rS, onf[:, :], ch[:, sl], start=first, stop=last))
                s.add("tensor", lambda e, sl=sl, rQ=rQ, sq=sq_f32, first=first, last=last: e.matmul(
                    rQ, onf[:, :], sq[:, sl], start=first, stop=last))

        def ln_rows():
            for th in range(2):
                sl = slice(th * 512, (th + 1) * 512)
                rS = pR1[0:1, :] if th == 0 else pR1[32:33, :]
                rQ = pR2[0:1, :] if th == 0 else pR2[32:33, :]
                s.add("vector", lambda e, sl=sl, rS=rS: e.tensor_scalar(MEAN[:, sl], rS, 1.0 / 2048.0, None, ALU.mult))
                s.add("vector", lambda e, sl=sl, rQ=rQ: e.tensor_scalar(T1[:, sl], rQ, 1.0 / 2048.0, None, ALU.mult))
            s.add("vector", lambda e: e.tensor_mul(T2[:, :], MEAN[:, :], MEAN[:, :]))
            s.add("vector", lambda e: e.tensor_sub(T1[:, :], T1[:, :], T2[:, :]))
            s.add("vector", lambda e: e.tensor_scalar(T1[:, :], T1[:, :], LN_EPS, None, ALU.add))
            s.add("scalar", lambda e: e.activation(T2[:, :], T1[:, :], AF.Ln, bias=0.0, scale=1.0))
            s.add("scalar", lambda e: e.activation(RSTD[:, :], T2[:, :], AF.Exp, bias=0.0, scale=-0.5))

        def ln_stats_dram_f32(src):
            for dc in range(NDC):
                cb = 3072 if dc % 2 == 0 else 4096
                qb = 0 if dc % 2 == 0 else 1024
                s.dma("sync", lambda e, dc=dc, cb=cb: e.dma_start(out=C[:, cb:cb + 1024], in_=src.ap()[dc, :, :]))
                eng_q = "gpsimd" if dc % 2 == 0 else "vector"
                s.add(eng_q, lambda e, cb=cb, qb=qb: e.tensor_mul(C[:, qb:qb + 1024], C[:, cb:cb + 1024], C[:, cb:cb + 1024]))
                stats_mm(C[:, cb:cb + 1024], C[:, qb:qb + 1024], dc == 0, dc == NDC - 1)
            ln_rows()

        def ln_bcast_ab():
            s.add("vector", lambda e: e.tensor_mul(T2[:, :], MEAN[:, :], RSTD[:, :]))
            for th in range(2):
                sl = slice(th * 512, (th + 1) * 512)
                s.add("tensor", lambda e, sl=sl: e.matmul(pR1[:, :], orf[:, :], RSTD[:, sl], start=True, stop=True))
                s.add("scalar", lambda e, sl=sl: e.activation(C[:, 0:1024][:, sl], pR1[:, :], AF.Identity, bias=0.0, scale=1.0))
                s.add("tensor", lambda e, sl=sl: e.matmul(pR1[:, :], orf[:, :], T2[:, sl], start=True, stop=True))
                s.add("scalar", lambda e, sl=sl: e.activation(C[:, 1024:2048][:, sl], pR1[:, :], AF.Identity, bias=0.0, scale=-1.0))

        def ln_apply(dst_fn, src):
            ln_bcast_ab()
            for dc in range(NDC):
                cb = 3072 if dc % 2 == 0 else 4096
                tb = 5120 if dc % 2 == 0 else 2048
                s.dma("sync", lambda e, dc=dc, cb=cb: e.dma_start(out=C[:, cb:cb + 1024], in_=src.ap()[dc, :, :]))
                eng_a = "vector" if dc % 2 == 0 else "gpsimd"
                s.add(eng_a, lambda e, cb=cb, tb=tb: e.tensor_mul(C[:, tb:tb + 1024], C[:, cb:cb + 1024], C[:, 0:1024]))
                s.add(eng_a, lambda e, dst=dst_fn(dc), tb=tb: e.tensor_add(dst, C[:, tb:tb + 1024], C[:, 1024:2048]))

        def ln_residual(sub_src, sub_is_f32=False, final=False):
            ln_bcast_ab()
            for dc in range(NDC):
                xb = 3072 if dc % 2 == 0 else 4096
                tb = 5120 if dc % 2 == 0 else 2048
                if sub_is_f32:
                    sb_c = 2048 if dc % 2 == 0 else 5120
                    s.dma("sync", lambda e, dc=dc, sb_c=sb_c: e.dma_start(out=C[:, sb_c:sb_c + 1024], in_=sub_src.ap()[dc, :, :]))
                    subchunk = C[:, sb_c:sb_c + 1024]
                else:
                    db_c = 10240 if dc % 2 == 0 else 11264
                    s.dma("sync", lambda e, dc=dc, db_c=db_c: e.dma_start(out=D[:, db_c:db_c + 1024], in_=sub_src.ap()[dc, :, :]))
                    subchunk = D[:, db_c:db_c + 1024]
                s.dma("sync", lambda e, dc=dc, xb=xb: e.dma_start(out=C[:, xb:xb + 1024], in_=x_dram.ap()[dc, :, :]))
                eng_r = "vector" if dc % 2 == 0 else "gpsimd"
                if sub_is_f32:
                    s.add(eng_r, lambda e, sc=subchunk: e.tensor_mul(sc, sc, C[:, 0:1024]))
                    s.add(eng_r, lambda e, sc=subchunk: e.tensor_add(sc, sc, C[:, 1024:2048]))
                    s.add(eng_r, lambda e, xb=xb, sc=subchunk: e.tensor_add(C[:, xb:xb + 1024], C[:, xb:xb + 1024], sc))
                else:
                    s.add(eng_r, lambda e, sc=subchunk, tb=tb: e.tensor_mul(C[:, tb:tb + 1024], sc, C[:, 0:1024]))
                    s.add(eng_r, lambda e, tb=tb: e.tensor_add(C[:, tb:tb + 1024], C[:, tb:tb + 1024], C[:, 1024:2048]))
                    s.add(eng_r, lambda e, xb=xb, tb=tb: e.tensor_add(C[:, xb:xb + 1024], C[:, xb:xb + 1024], C[:, tb:tb + 1024]))
                tgt = out_final if final else x_dram.ap()
                s.dma("sync", lambda e, dc=dc, tgt=tgt, xb=xb: e.dma_start(out=tgt[dc, :, :], in_=C[:, xb:xb + 1024]))

        def ln_stats_dram_bf(src2d):
            for dc in range(NDC):
                s.dma("sync", lambda e, dc=dc: e.dma_start(out=D[:, 10240:11264], in_=src2d.ap()[dc * 128:(dc + 1) * 128, :]))
                s.add("vector", lambda e: e.tensor_copy(C[:, 3072:4096], D[:, 10240:11264]))
                s.add("gpsimd", lambda e: e.tensor_mul(C[:, 4096:5120], C[:, 3072:4096], C[:, 3072:4096]))
                stats_mm(C[:, 3072:4096], C[:, 4096:5120], dc == 0, dc == NDC - 1)
            ln_rows()

        def ln_residual_bf2d(src2d, final=False):
            ln_bcast_ab()
            for dc in range(NDC):
                s.dma("sync", lambda e, dc=dc: e.dma_start(out=D[:, 10240:11264], in_=src2d.ap()[dc * 128:(dc + 1) * 128, :]))
                s.dma("sync", lambda e, dc=dc: e.dma_start(out=C[:, 3072:4096], in_=x_dram.ap()[dc, :, :]))
                s.add("vector", lambda e: e.tensor_mul(C[:, 5120:6144], D[:, 10240:11264], C[:, 0:1024]))
                s.add("vector", lambda e: e.tensor_add(C[:, 5120:6144], C[:, 5120:6144], C[:, 1024:2048]))
                s.add("gpsimd", lambda e: e.tensor_add(C[:, 3072:4096], C[:, 3072:4096], C[:, 5120:6144]))
                tgt = out_final if final else x_dram.ap()
                s.dma("sync", lambda e, dc=dc, tgt=tgt: e.dma_start(out=tgt[dc, :, :], in_=C[:, 3072:4096]))

        def dump_dbg_bf(ap_fn):
            for dc in range(NDC):
                s.add("vector", lambda e, dc=dc: e.tensor_copy(C[:, 3072:4096], ap_fn(dc)))
                s.dma("sync", lambda e, dc=dc: e.dma_start(out=dbg[dc, :, :], in_=C[:, 3072:4096]))
            s.add("gpsimd", lambda e: e.engine_nop())
            s.emit(nc, st)

        WOFF, XHOFF, QFOFF = 0, 32768, 49152

        def xh_ap(dc):
            return AB[:, XHOFF + dc * TOK: XHOFF + (dc + 1) * TOK]

        ln_stats_dram_f32(x_dram)
        ln_apply(xh_ap, x_dram)
        if debug_tap == "xh1":
            dump_dbg_bf(xh_ap)
            return nc

        def recon_gen(wkey, slab_in, ncol_row):
            Q = spl[wkey + "_Q"]
            PH = spl[wkey + "_phi"]
            for k in range(5):
                s.dma("sync", lambda e, k=k, Q=Q: e.dma_start(
                    out=C[:, k * 128:(k + 1) * 128], in_=Q[k:k + 1, :].rearrange("o (p f) -> (o p) f", p=128)))
            for k in range(3):
                s.dma("sync", lambda e, k=k, PH=PH: e.dma_start(
                    out=C[:, 640 + k * 128: 640 + (k + 1) * 128], in_=PH[k:k + 1, :].rearrange("o (p f) -> (o p) f", p=128)))
            Q0, Q1, Q2, Q3, Q4 = (C[:, k * 128:(k + 1) * 128] for k in range(5))
            P1_, P2_, P3_ = (C[:, 640 + k * 128: 640 + (k + 1) * 128] for k in range(3))

            def rowt(i):
                return C[:, 1024 + i * 128: 1024 + (i + 1) * 128]
            a1, a2, a3, bb2, bb3, ee0, ee1, t0, t1, al0, al1, al2, um1, zz0, on1 = (rowt(i) for i in range(15))
            v = lambda f: s.add("vector", f)
            v(lambda e: e.tensor_sub(a1, Q2, Q0)); v(lambda e: e.tensor_scalar_mul(a1, a1, 0.5))
            v(lambda e: e.scalar_tensor_tensor(t0, Q1, -2.5, Q0, ALU.mult, ALU.add))
            v(lambda e: e.scalar_tensor_tensor(t1, Q2, 2.0, t0, ALU.mult, ALU.add))
            v(lambda e: e.scalar_tensor_tensor(a2, Q3, -0.5, t1, ALU.mult, ALU.add))
            v(lambda e: e.tensor_scalar_mul(t0, Q0, -0.5))
            v(lambda e: e.scalar_tensor_tensor(t1, Q1, 1.5, t0, ALU.mult, ALU.add))
            v(lambda e: e.scalar_tensor_tensor(t0, Q2, -1.5, t1, ALU.mult, ALU.add))
            v(lambda e: e.scalar_tensor_tensor(a3, Q3, 0.5, t0, ALU.mult, ALU.add))
            v(lambda e: e.scalar_tensor_tensor(t0, Q2, -2.5, Q1, ALU.mult, ALU.add))
            v(lambda e: e.scalar_tensor_tensor(t1, Q3, 2.0, t0, ALU.mult, ALU.add))
            v(lambda e: e.scalar_tensor_tensor(bb2, Q4, -0.5, t1, ALU.mult, ALU.add))
            v(lambda e: e.tensor_scalar_mul(t0, Q1, -0.5))
            v(lambda e: e.scalar_tensor_tensor(t1, Q2, 1.5, t0, ALU.mult, ALU.add))
            v(lambda e: e.scalar_tensor_tensor(t0, Q3, -1.5, t1, ALU.mult, ALU.add))
            v(lambda e: e.scalar_tensor_tensor(bb3, Q4, 0.5, t0, ALU.mult, ALU.add))
            v(lambda e: e.tensor_sub(ee1, bb3, a3))
            v(lambda e: e.tensor_sub(t0, bb2, bb3)); v(lambda e: e.tensor_sub(t0, t0, a2))
            v(lambda e: e.scalar_tensor_tensor(ee0, a3, -2.0, t0, ALU.mult, ALU.add))
            v(lambda e: e.tensor_mul(t0, a1, P1_)); v(lambda e: e.tensor_add(al0, Q1, t0))
            v(lambda e: e.tensor_mul(t0, a2, P2_)); v(lambda e: e.tensor_add(al0, al0, t0))
            v(lambda e: e.tensor_mul(t0, a3, P3_)); v(lambda e: e.tensor_add(al0, al0, t0))
            v(lambda e: e.tensor_mul(t0, a2, P1_)); v(lambda e: e.scalar_tensor_tensor(al1, t0, 2.0, a1, ALU.mult, ALU.add))
            v(lambda e: e.tensor_mul(t0, a3, P2_)); v(lambda e: e.scalar_tensor_tensor(al1, t0, 3.0, al1, ALU.mult, ALU.add))
            v(lambda e: e.tensor_mul(t0, a3, P1_)); v(lambda e: e.scalar_tensor_tensor(al2, t0, 3.0, a2, ALU.mult, ALU.add))
            v(lambda e: e.tensor_mul(t0, ee1, P1_)); v(lambda e: e.tensor_add(zz0, ee0, t0))
            v(lambda e: e.tensor_scalar(um1, P1_, -1.0, None, ALU.add))
            v(lambda e: e.memset(on1, 1.0))
            for (ri, ap) in ((0, al0), (1, al1), (2, al2), (3, a3), (4, um1), (5, on1), (6, zz0), (7, ee1)):
                s.dma("sync", lambda e, ri=ri, ap=ap: e.dma_start(
                    out=rows_dram.ap()[ri, :].rearrange("(p f) -> p f", p=128), in_=ap))
            yield
            for ti in range(CPC // 512):
                yield
                par = ti % 2
                r4 = rr4[:, par * 512:(par + 1) * 512]
                r2u = rr2[:, par * 1024: par * 1024 + 512]
                r2z = rr2[:, par * 1024 + 512: (par + 1) * 1024]
                pw, pu = (pF, pR1)
                csl = slice(ti * 512, (ti + 1) * 512)
                s.dma("sync", lambda e, csl=csl, r4=r4: e.dma_start(out=r4, in_=rows_dram.ap()[0:4, csl]))
                s.dma("sync", lambda e, csl=csl, r2u=r2u: e.dma_start(out=r2u, in_=rows_dram.ap()[4:6, csl]))
                s.dma("sync", lambda e, csl=csl, r2z=r2z: e.dma_start(out=r2z, in_=rows_dram.ap()[6:8, csl]))
                s.add("tensor", lambda e, pw=pw, r4=r4: e.matmul(pw[:, :], b4s[:, :], r4, start=True, stop=True))
                s.add("tensor", lambda e, pu=pu, r2u=r2u: e.matmul(pu[:, :], b2s[:, :], r2u, start=True, stop=True))
                rbuf = C[:, 3072:3584] if par else C[:, 4096:4608]
                r2buf = C[:, 3584:4096] if par else C[:, 4608:5120]
                wbuf = D[:, 11776:12288] if par else D[:, 10752:11264]
                s.add("scalar", lambda e, pu=pu, rbuf=rbuf: e.activation(rbuf, pu[:, :], AF.Relu, bias=0.0, scale=1.0))
                s.add("vector", lambda e, pu=pu, rbuf=rbuf, r2buf=r2buf: e.tensor_mul(r2buf, rbuf, pu[:, :]))
                s.add("tensor", lambda e, pu=pu, r2z=r2z: e.matmul(pu[:, :], b2s[:, :], r2z, start=True, stop=True))
                s.add("vector", lambda e, pu=pu, rbuf=rbuf, r2buf=r2buf: e.tensor_mul(rbuf, r2buf, pu[:, :]))
                s.add("vector", lambda e, pw=pw, rbuf=rbuf, wbuf=wbuf: e.tensor_add(wbuf, pw[:, :], rbuf))
                col0 = ti * 512
                b = col0 // ncol_row
                o = col0 % ncol_row
                s.dma("sync", lambda e, b=b, o=o, wbuf=wbuf: e.dma_start(
                    out=slab_in.ap()[b * 128:(b + 1) * 128, o:o + 512], in_=wbuf))


        def l0_dequant(which, base=None):
            codes, scales = l0c[which], l0s[which]
            base = WOFF if base is None else base
            for dc in range(NDC):
                cb = 0 if dc % 2 == 0 else 4096
                dst = AB[:, base + dc * D_MODEL: base + (dc + 1) * D_MODEL]
                s.dma("gpsimd", lambda e, dc=dc, codes=codes, cb=cb: e.dma_start(
                    out=D[:, cb:cb + 2048], in_=codes[dc * 128:(dc + 1) * 128, :]))
                s.dma("gpsimd", lambda e, dc=dc, scales=scales, cb=cb: e.dma_start(
                    out=D[:, cb + 2048:cb + 4096], in_=scales[dc * 128:(dc + 1) * 128, :]))
                eng_d = "vector" if dc % 2 == 0 else "gpsimd"
                s.add(eng_d, lambda e, dst=dst, cb=cb: e.tensor_mul(dst, D[:, cb:cb + 2048], D[:, cb + 2048:cb + 4096]))

        def l0_project(out_kind, bias_idx):
            for oc in range(NDC):
                pair = [(pD, pF), (pA, pB)][oc % 2]
                for th in range(2):
                    ps = pair[th]
                    for dc in range(NDC):
                        s.add("tensor", lambda e, dc=dc, oc=oc, th=th, ps=ps: e.matmul(
                            ps[:, :], AB[:, WOFF + dc * D_MODEL + oc * 128: WOFF + dc * D_MODEL + (oc + 1) * 128],
                            xh_ap(dc)[:, th * 512:(th + 1) * 512],
                            start=(dc == 0), stop=(dc == NDC - 1)))
                bcol = biasq[:, bias_idx * NDC + oc: bias_idx * NDC + oc + 1]
                if out_kind == "q":
                    for th in range(2):
                        s.add("scalar", lambda e, oc=oc, bcol=bcol, th=th, ps=pair[th]: e.activation(
                            AB[:, QFOFF + oc * TOK + th * 512: QFOFF + oc * TOK + (th + 1) * 512], ps[:, :],
                            AF.Identity, bias=bcol, scale=1.0))
                else:
                    for th in range(2):
                        s.add("scalar", lambda e, bcol=bcol, th=th, ps=pair[th]: e.activation(
                            D[:, 8192 + th * 512:8704 + th * 512], ps[:, :], AF.Identity, bias=bcol, scale=1.0))
                    if out_kind == "k":
                        s.dma("sync", lambda e, oc=oc: e.dma_start(out=k_ag_in.ap()[oc * 128:(oc + 1) * 128, :], in_=D[:, 8192:9216]))
                    else:
                        for tt in range(8):
                            s.add("tensor", lambda e, tt=tt: e.transpose(
                                pT[:, 0:128], D[:, 8192 + tt * 128: 8320 + tt * 128], idb[:, :]))
                            s.add("vector", lambda e, tt=tt: e.tensor_copy(D[:, 9216 + tt * 128: 9344 + tt * 128], pT[:, 0:128]))
                        for tt in range(8):
                            s.dma("sync", lambda e, oc=oc, tt=tt: e.dma_start(
                                out=v_ag_in.ap()[oc * 128:(oc + 1) * 128, tt * 128:(tt + 1) * 128],
                                in_=D[:, 9216 + tt * 128: 9344 + tt * 128]))

        l0_dequant("q"); l0_project("q", 0)
        l0_dequant("k"); l0_project("k", 1)
        l0_dequant("v"); l0_project("v", 2)
        s.add("gpsimd", lambda e: e.collective_compute(
            "AllGather", ALU.bypass, replica_groups=GROUPS,
            ins=[k_ag_in.ap()], outs=[k_ag_out.ap()]), inc=1)
        s.add("gpsimd", lambda e: e.collective_compute(
            "AllGather", ALU.bypass, replica_groups=GROUPS,
            ins=[v_ag_in.ap()], outs=[v_ag_out.ap()]), inc=1)
        def tern_dequant_gen(nat_src, o_rows, i_cols, dst_tile_fn):
            ng = i_cols // 128
            for ot in range(o_rows // 128):
                yield
                op_ = ot % 2
                cb = 0 if op_ == 0 else 3072
                eb = 32 if op_ == 0 else 96
                s.dma("sync", lambda e, ot=ot, cb=cb: e.dma_start(out=C[:, cb:cb + i_cols], in_=nat_src[ot * 128:(ot + 1) * 128, :]))
                s.add("vector", lambda e, ng=ng, cb=cb, eb=eb: e.tensor_reduce(
                    E[:, eb:eb + ng], C[:, cb:cb + i_cols].rearrange("p (g j) -> p g j", j=128),
                    axis=AX.X, op=ALU.add, apply_absolute_value=True))
                s.add("vector", lambda e, ng=ng, eb=eb: e.tensor_scalar(E[:, eb:eb + ng], E[:, eb:eb + ng], 1.0 / 128.0, TERN_EPS, ALU.mult, ALU.max))
                s.add("vector", lambda e, ng=ng, eb=eb: e.tensor_scalar(E[:, eb + 32:eb + 32 + ng], E[:, eb:eb + ng], 0.5, None, ALU.mult))
                for g in range(ng):
                    gp = g % 2
                    eng_t = "vector" if gp == 0 else "gpsimd"
                    qb = 2048 + gp * 128 if op_ == 0 else 5120 + gp * 128
                    db_ = 11264 + (2 * op_ + gp) * 128
                    w_t = C[:, cb + g * 128: cb + (g + 1) * 128]
                    s.add(eng_t, lambda e, w_t=w_t, g=g, qb=qb, eb=eb: e.tensor_single_scalar(
                        C[:, qb:qb + 128], w_t, E[:, eb + 32 + g:eb + 33 + g], ALU.is_gt))
                    nb = 2432 + gp * 128 if op_ == 0 else 5504 + gp * 128
                    s.add(eng_t, lambda e, w_t=w_t, g=g, nb=nb, eb=eb: e.tensor_scalar(
                        C[:, nb:nb + 128], w_t, -1.0, E[:, eb + 32 + g:eb + 33 + g], ALU.mult, ALU.is_gt))
                    s.add(eng_t, lambda e, qb=qb, nb=nb: e.tensor_sub(C[:, qb:qb + 128], C[:, qb:qb + 128], C[:, nb:nb + 128]))
                    s.add(eng_t, lambda e, g=g, qb=qb, db_=db_, eb=eb: e.tensor_scalar_mul(D[:, db_:db_ + 128], C[:, qb:qb + 128], E[:, eb + g:eb + 1 + g]))
                    s.add("tensor", lambda e, db_=db_, gp=gp: e.transpose(pT[:, gp * 128:(gp + 1) * 128], D[:, db_:db_ + 128], idb[:, :]))
                    s.add("vector", lambda e, g=g, ot=ot, gp=gp: e.tensor_copy(dst_tile_fn(g, ot), pT[:, gp * 128:(gp + 1) * 128]))

        def _fill_gens():
            for pi in range(4):
                nat = (l1q_nat, l1k_nat, l1v_nat, l1o_nat)[pi]
                if pi < 3:
                    yield from tern_dequant_gen(nat, 512, D_MODEL,
                                                lambda ic, ot: D[:, ic * 512 + ot * 128: ic * 512 + (ot + 1) * 128])
                else:
                    yield from tern_dequant_gen(nat, D_MODEL, 512,
                                                lambda ic, ot: D[:, ic * 2048 + ot * 128: ic * 2048 + (ot + 1) * 128])
                s.dma("sync", lambda e, pi=pi: e.dma_start(out=l1w_sc.ap()[pi], in_=D[:, 0:8192]))
            for (wk, si, nr) in (("g", g_sl_in, D_MODEL), ("u", u_sl_in, D_MODEL), ("d", d_sl_in, INTER)):
                yield from recon_gen(wk, si, nr)
            for (si, so) in ((g_sl_in, g_sl), (u_sl_in, u_sl), (d_sl_in, d_sl)):
                s.add("gpsimd", lambda e, si=si, so=so: e.collective_compute(
                    "AllGather", ALU.bypass, replica_groups=[[0, 1, 2, 3, 4, 5, 6, 7]],
                    ins=[si.ap()], outs=[so.ap()]), inc=1)
            yield

        _gen = _fill_gens()
        _pre = []
        for e_ in range(NEXP):
            for (dst, srcw) in ((w1B, w1R), (w3B, w3R), (w2B, w2R)):
                _pre.append((dst.ap()[e_], srcw[e_]))
        _pre.append((sw1B.ap(), sw1R))
        _pre.append((sw3B.ap(), sw3R))
        _pre.append((sw2B.ap(), sw2R))
        _state = {"gen_done": False, "pre_i": 0, "pre_on": False}

        def pump(k=1):
            for _ in range(k):
                if not _state["gen_done"]:
                    try:
                        next(_gen)
                    except StopIteration:
                        _state["gen_done"] = True
                if _state["pre_on"] and _state["pre_i"] < len(_pre):
                    dst, srcw = _pre[_state["pre_i"]]
                    _state["pre_i"] += 1
                    s.dma("gpsimd", lambda e, dst=dst, srcw=srcw: e.dma_start(
                        out=dst.rearrange("a p f -> (a p) f"), in_=srcw.rearrange("a p f -> (a p) f")))

        def pump_drain():
            _state["pre_on"] = True
            while not _state["gen_done"] or _state["pre_i"] < len(_pre):
                pump(1)
        _state["pre_on"] = True

        if debug_tap == "qkv0":
            dump_dbg_bf(lambda oc: AB[:, QFOFF + oc * TOK: QFOFF + (oc + 1) * TOK])
            return nc

        ATOFF = 0
        KVBUF = 16384
        for h in range(NH):
            kb = KVBUF + (h % 2) * 8192
            for r in range(4):
                s.dma("sync", lambda e, h=h, r=r, kb=kb: e.dma_start(
                    out=AB[:, kb + r * TOK: kb + (r + 1) * TOK],
                    in_=k_ag_out.ap()[r * D_MODEL + h * 128: r * D_MODEL + (h + 1) * 128, :]))
                s.dma("gpsimd", lambda e, h=h, r=r, kb=kb: e.dma_start(
                    out=AB[:, kb + 4096 + r * TOK: kb + 4096 + (r + 1) * TOK],
                    in_=v_ag_out.ap()[r * D_MODEL + h * 128: r * D_MODEL + (h + 1) * 128, :]))
            def emit_pvpair(kt, kb=kb):
                pr = kt % 2
                ptsp = D[:, 8192 + pr * 1024: 8192 + (pr + 1) * 1024]
                s.add("tensor", lambda e, kt=kt, ptsp=ptsp, kb=kb: e.matmul(
                    pC[:, :], AB[:, kb + 4096 + kt * 128: kb + 4096 + (kt + 1) * 128], ptsp[:, 0:512],
                    start=(kt == 0), stop=(kt == 31)))
                s.add("tensor", lambda e, kt=kt, ptsp=ptsp, kb=kb: e.matmul(
                    pD[:, :], AB[:, kb + 4096 + kt * 128: kb + 4096 + (kt + 1) * 128], ptsp[:, 512:1024],
                    start=(kt == 0), stop=(kt == 31)))
                s.add("tensor", lambda e, kt=kt, ptsp=ptsp: e.matmul(
                    pR2[0:1, :], onb[:, :], ptsp[:, 0:512], start=(kt == 0), stop=(kt == 31)))
                s.add("tensor", lambda e, kt=kt, ptsp=ptsp: e.matmul(
                    pR2[32:33, :], onb[:, :], ptsp[:, 512:1024], start=(kt == 0), stop=(kt == 31)))
            for kt in range(32):
                pr = kt % 2
                ptsp = D[:, 8192 + pr * 1024: 8192 + (pr + 1) * 1024]
                for qc in range(2):
                    s.add("tensor", lambda e, h=h, kt=kt, qc=qc, kb=kb: e.matmul(
                        pAB[:, qc * 512:(qc + 1) * 512], AB[:, kb + kt * 128: kb + (kt + 1) * 128],
                        AB[:, QFOFF + h * TOK: QFOFF + (h + 1) * TOK][:, qc * 512:(qc + 1) * 512],
                        start=True, stop=True))
                s.add("scalar", lambda e, ptsp=ptsp: e.activation(
                    ptsp, pAB[:, 0:1024], AF.Exp, bias=shc[:, 0:1], scale=C_ATT))
                if kt > 0:
                    emit_pvpair(kt - 1)
            emit_pvpair(31)
            for qc in range(2):
                lrow = pR2[0:1, :] if qc == 0 else pR2[32:33, :]
                pvac = pC if qc == 0 else pD
                tslot = E[0:1, qc * 512:(qc + 1) * 512]
                bslot = slice(5120 + qc * 512, 5632 + qc * 512)
                bps = pAB[:, qc * 512:(qc + 1) * 512]
                s.add("vector", lambda e, lrow=lrow, tslot=tslot: e.reciprocal(tslot, lrow))
                s.add("tensor", lambda e, tslot=tslot, bps=bps: e.matmul(bps, orf[:, :], tslot, start=True, stop=True))
                s.add("scalar", lambda e, bslot=bslot, bps=bps: e.activation(C[:, bslot], bps, AF.Identity, bias=0.0, scale=1.0))
                s.add("vector", lambda e, h=h, qc=qc, pvac=pvac, bslot=bslot: e.tensor_mul(
                    AB[:, ATOFF + h * TOK + qc * 512: ATOFF + h * TOK + (qc + 1) * 512], pvac[:, :], C[:, bslot]))
            pump(10)

        if debug_tap == "attn0":
            dump_dbg_bf(lambda oc: AB[:, ATOFF + oc * TOK: ATOFF + (oc + 1) * TOK])
            return nc

        WO2 = 16384
        l0_dequant("o", base=WO2)
        for oc in range(NDC):
            pump(3)
            pair = [(pD, pF), (pA, pB)][oc % 2]
            for th in range(2):
                ps = pair[th]
                for dc in range(NDC):
                    s.add("tensor", lambda e, dc=dc, oc=oc, th=th, ps=ps: e.matmul(
                        ps[:, :], AB[:, WO2 + dc * D_MODEL + oc * 128: WO2 + dc * D_MODEL + (oc + 1) * 128],
                        AB[:, ATOFF + dc * TOK: ATOFF + (dc + 1) * TOK][:, th * 512:(th + 1) * 512],
                        start=(dc == 0), stop=(dc == NDC - 1)))
            bcol = biasq[:, 3 * NDC + oc: 3 * NDC + oc + 1]
            for th in range(2):
                s.add("scalar", lambda e, bcol=bcol, th=th, ps=pair[th]: e.activation(
                    C[:, 3072 + th * 512:3584 + th * 512], ps[:, :], AF.Identity, bias=bcol, scale=1.0))
            s.add("gpsimd", lambda e: e.tensor_mul(C[:, 4096:5120], C[:, 3072:4096], C[:, 3072:4096]))
            stats_mm(C[:, 3072:4096], C[:, 4096:5120], oc == 0, oc == NDC - 1)
            s.dma("gpsimd", lambda e, oc=oc: e.dma_start(out=sub_dram.ap()[oc, :, :], in_=C[:, 3072:4096]))
        ln_rows()
        ln_residual(sub_dram)
        pump_drain()
        if debug_tap == "x1":
            s.dma("sync", lambda e: e.dma_start(out=dbg[:, :, :], in_=x_dram.ap()[:, :, :]))
            s.add("gpsimd", lambda e: e.engine_nop())
            s.emit(nc, st)
            return nc

        ln_stats_dram_f32(x_dram)
        ln_apply(xh_ap, x_dram)
        for th in range(2):
            for dc in range(NDC):
                s.add("tensor", lambda e, dc=dc, th=th: e.matmul(
                    pR2[0:8, :], gtb[:, dc * NEXP:(dc + 1) * NEXP],
                    xh_ap(dc)[:, th * 512:(th + 1) * 512], start=(dc == 0), stop=(dc == NDC - 1)))
            s.add("scalar", lambda e, th=th: e.activation(C[0:8, 4096 + th * 512:4608 + th * 512], pR2[0:8, :], AF.Sigmoid, bias=0.0, scale=1.0))
        s.add("vector", lambda e: e.tensor_scalar(C[0:8, 5120:6144], C[0:8, 4096:5120], ebc[:, 0:1], None, ALU.add))
        for tt in range(8):
            s.add("tensor", lambda e, tt=tt: e.transpose(pR1[:, 0:8], C[0:8, 5120 + tt * 128:5248 + tt * 128], idf[0:8, 0:8]))
            s.add("vector", lambda e: e.tensor_copy(C[:, 2048:2056], pR1[:, 0:8]))
            s.add("tensor", lambda e, tt=tt: e.transpose(pR1[:, 8:16], C[0:8, 4096 + tt * 128:4224 + tt * 128], idf[0:8, 0:8]))
            s.add("vector", lambda e: e.tensor_copy(C[:, 2056:2064], pR1[:, 8:16]))
            s.add("vector", lambda e: e.reduce_max(C[:, 2064:2065], C[:, 2048:2056], axis=AX.X))
            s.add("vector", lambda e: e.tensor_single_scalar(C[:, 2072:2080], C[:, 2048:2056], C[:, 2064:2065], ALU.is_equal))
            s.add("vector", lambda e: e.scalar_tensor_tensor(C[:, 2080:2088], C[:, 2072:2080], -1e30, C[:, 2048:2056], ALU.mult, ALU.add))
            s.add("vector", lambda e: e.reduce_max(C[:, 2065:2066], C[:, 2080:2088], axis=AX.X))
            s.add("vector", lambda e: e.tensor_single_scalar(C[:, 2072:2080], C[:, 2048:2056], C[:, 2065:2066], ALU.is_ge))
            s.add("vector", lambda e: e.tensor_mul(C[:, 2072:2080], C[:, 2072:2080], C[:, 2056:2064]))
            s.add("tensor", lambda e: e.transpose(pR1[0:8, 0:128], C[:, 2072:2080], idf[:, :]))
            s.add("vector", lambda e, tt=tt: e.tensor_copy(C[0:8, tt * 128:(tt + 1) * 128], pR1[0:8, 0:128]))
        if debug_tap == "ms":
            s.add("vector", lambda e: e.tensor_copy(C[0:8, 2048:3072], C[0:8, 0:1024]))
            s.dma("sync", lambda e: e.dma_start(out=dbg[0, 0:8, :], in_=C[0:8, 2048:3072]))
            s.dma("sync", lambda e: e.dma_start(out=dbg[1, 0:8, :], in_=C[0:8, 4096:5120]))
            s.dma("sync", lambda e: e.dma_start(out=dbg[2, 0:8, :], in_=C[0:8, 5120:6144]))
            s.add("gpsimd", lambda e: e.engine_nop())
            s.emit(nc, st)
            return nc
        XEOFF = 49152
        for th in range(2):
            tsl = slice(th * 512, (th + 1) * 512)
            MSB = 61440
            for ei in range(8):
                s.add("tensor", lambda e, ei=ei, tsl=tsl: e.matmul(
                    pR1[:, :], slb[:, ei * 128:(ei + 1) * 128], C[0:8, 0:1024][:, tsl], start=True, stop=True))
                s.add("scalar", lambda e, ei=ei: e.activation(AB[:, MSB + ei * 512: MSB + (ei + 1) * 512],
                                                              pR1[:, :], AF.Identity, bias=0.0, scale=1.0))
            for ei in range(9):
                he_off = ei * 8 * 512 if ei < 8 else 57344
                if ei < 8:
                    for dc in range(NDC):
                        eng_x = "vector" if dc % 2 == 0 else "gpsimd"
                        s.add(eng_x, lambda e, dc=dc, tsl=tsl, ei=ei: e.tensor_mul(
                            AB[:, XEOFF + dc * 512: XEOFF + (dc + 1) * 512], xh_ap(dc)[:, tsl],
                            AB[:, MSB + ei * 512: MSB + (ei + 1) * 512]))
                    xe_list = [AB[:, XEOFF + dc * 512: XEOFF + (dc + 1) * 512] for dc in range(NDC)]
                else:
                    xe_list = [xh_ap(dc)[:, tsl] for dc in range(NDC)]
                w1_src = w1B.ap()[ei] if ei < 8 else sw1B.ap()
                w3_src = w3B.ap()[ei] if ei < 8 else sw3B.ap()
                for hidc in range(8):
                    w1b = 512 if hidc % 2 == 0 else 5120
                    w3b = 2560 if hidc % 2 == 0 else 7168
                    s.dma("sync", lambda e, w=w1_src, hidc=hidc, w1b=w1b: e.dma_start(
                        out=D[:, w1b:w1b + 2048], in_=w[hidc]))
                    s.dma("gpsimd", lambda e, w=w3_src, hidc=hidc, w3b=w3b: e.dma_start(
                        out=D[:, w3b:w3b + 2048], in_=w[hidc]))
                    u1p = pA if hidc % 2 == 0 else pC
                    u3p = pB if hidc % 2 == 0 else pF
                    for dc in range(NDC):
                        s.add("tensor", lambda e, dc=dc, xe=xe_list[dc], w1b=w1b, u1p=u1p: e.matmul(
                            u1p[:, :], D[:, w1b + dc * 128: w1b + (dc + 1) * 128], xe,
                            start=(dc == 0), stop=(dc == NDC - 1)))
                        s.add("tensor", lambda e, dc=dc, xe=xe_list[dc], w3b=w3b, u3p=u3p: e.matmul(
                            u3p[:, :], D[:, w3b + dc * 128: w3b + (dc + 1) * 128], xe,
                            start=(dc == 0), stop=(dc == NDC - 1)))
                    s.add("scalar", lambda e, u1p=u1p: e.activation(D[:, 4608:5120], u1p[:, :], AF.Silu, bias=0.0, scale=1.0))
                    s.add("vector", lambda e, he_off=he_off, hidc=hidc, u3p=u3p: e.tensor_mul(
                        AB[:, he_off + hidc * 512: he_off + (hidc + 1) * 512], u3p[:, :], D[:, 4608:5120]))
            for dc2 in range(NDC):
                dacc = pC if dc2 % 2 == 0 else pD
                cnt = 0
                for ei in range(9):
                    w2_src = w2B.ap()[ei] if ei < 8 else sw2B.ap()
                    he_off = ei * 8 * 512 if ei < 8 else 57344
                    wb = 512 + ((dc2 * 9 + ei) % 2) * 1024
                    s.dma("sync" if ei % 2 == 0 else "gpsimd", lambda e, w=w2_src, dc2=dc2, wb=wb: e.dma_start(
                        out=D[:, wb:wb + 1024], in_=w[dc2]))
                    for hc in range(8):
                        s.add("tensor", lambda e, hc=hc, he_off=he_off, cnt=cnt, wb=wb, dacc=dacc: e.matmul(
                            dacc[:, :], D[:, wb + hc * 128: wb + (hc + 1) * 128],
                            AB[:, he_off + hc * 512: he_off + (hc + 1) * 512],
                            start=(cnt == 0), stop=(cnt == 71)))
                        cnt += 1
                ch = C[:, 3072:4096][:, tsl]
                s.add("scalar", lambda e, ch=ch, dacc=dacc: e.activation(ch, dacc[:, :], AF.Identity, bias=0.0, scale=1.0))
                s.add("gpsimd", lambda e, ch=ch: e.tensor_mul(C[:, 4096:4608], ch, ch))
                rS = pR1[0:1, :] if th == 0 else pR1[32:33, :]
                rQ = pR2[0:1, :] if th == 0 else pR2[32:33, :]
                s.add("tensor", lambda e, ch=ch, dc2=dc2, rS=rS: e.matmul(
                    rS, onf[:, :], ch, start=(dc2 == 0), stop=(dc2 == NDC - 1)))
                s.add("tensor", lambda e, dc2=dc2, rQ=rQ: e.matmul(
                    rQ, onf[:, :], C[:, 4096:4608], start=(dc2 == 0), stop=(dc2 == NDC - 1)))
                s.dma("gpsimd", lambda e, dc2=dc2, th=th, ch=ch: e.dma_start(
                    out=sub_dram.ap()[dc2, :, th * 512:(th + 1) * 512], in_=ch))
            tslr = slice(th * 512, (th + 1) * 512)
            rS = pR1[0:1, :] if th == 0 else pR1[32:33, :]
            rQ = pR2[0:1, :] if th == 0 else pR2[32:33, :]
            s.add("vector", lambda e, tslr=tslr, rS=rS: e.tensor_scalar(MEAN[:, tslr], rS, 1.0 / 2048.0, None, ALU.mult))
            s.add("vector", lambda e, tslr=tslr, rQ=rQ: e.tensor_scalar(T1[:, tslr], rQ, 1.0 / 2048.0, None, ALU.mult))
        if debug_tap == "moe":
            for dc in range(NDC):
                s.dma("sync", lambda e, dc=dc: e.dma_start(out=D[:, 10240:11264], in_=sub_dram.ap()[dc, :, :]))
                s.add("vector", lambda e: e.tensor_copy(C[:, 3072:4096], D[:, 10240:11264]))
                s.dma("sync", lambda e, dc=dc: e.dma_start(out=dbg[dc, :, :], in_=C[:, 3072:4096]))
            s.add("gpsimd", lambda e: e.engine_nop())
            s.emit(nc, st)
            return nc
        s.add("vector", lambda e: e.tensor_mul(T2[:, :], MEAN[:, :], MEAN[:, :]))
        s.add("vector", lambda e: e.tensor_sub(T1[:, :], T1[:, :], T2[:, :]))
        s.add("vector", lambda e: e.tensor_scalar(T1[:, :], T1[:, :], LN_EPS, None, ALU.add))
        s.add("scalar", lambda e: e.activation(T2[:, :], T1[:, :], AF.Ln, bias=0.0, scale=1.0))
        s.add("scalar", lambda e: e.activation(RSTD[:, :], T2[:, :], AF.Exp, bias=0.0, scale=-0.5))
        ln_residual(sub_dram)
        if debug_tap == "x2":
            s.dma("sync", lambda e: e.dma_start(out=dbg[:, :, :], in_=x_dram.ap()[:, :, :]))
            s.add("gpsimd", lambda e: e.engine_nop())
            s.emit(nc, st)
            return nc

        ln_stats_dram_f32(x_dram)
        ln_apply(xh_ap, x_dram)
        for dc in range(NDC):
            s.dma("sync", lambda e, dc=dc: e.dma_start(out=xh_ag_in.ap()[dc * 128:(dc + 1) * 128, :], in_=xh_ap(dc)))
        s.add("gpsimd", lambda e: e.collective_compute(
            "AllGather", ALU.bypass, replica_groups=GROUPS,
            ins=[xh_ag_in.ap()], outs=[xh_ag_out.ap()]), inc=1)

        QO, KO, VO, XBO = 0, 16384, 32768, 49152
        for rb in range(4):
            for dc in range(NDC):
                s.dma("sync", lambda e, rb=rb, dc=dc: e.dma_start(
                    out=AB[:, XBO + dc * TOK: XBO + (dc + 1) * TOK],
                    in_=xh_ag_out.ap()[rb * D_MODEL + dc * 128: rb * D_MODEL + (dc + 1) * 128, :]))
            for (pi, dsto) in ((0, QO), (1, KO), (2, VO)):
                s.dma("sync", lambda e, pi=pi: e.dma_start(out=D[:, 0:4096], in_=l1w_sc.ap()[pi, :, 0:4096]))
                s.dma("gpsimd", lambda e, pi=pi: e.dma_start(out=D[:, 4096:8192], in_=l1w_sc.ap()[pi, :, 4096:8192]))
                for fc in range(4):
                    pair = [(pD, pF), (pA, pB)][fc % 2]
                    for th in range(2):
                        ps = pair[th]
                        for dc in range(NDC):
                            s.add("tensor", lambda e, dc=dc, fc=fc, ps=ps, th=th: e.matmul(
                                ps[:, :], D[:, dc * 512 + fc * 128: dc * 512 + (fc + 1) * 128],
                                AB[:, XBO + dc * TOK: XBO + (dc + 1) * TOK][:, th * 512:(th + 1) * 512],
                                start=(dc == 0), stop=(dc == NDC - 1)))
                    for th in range(2):
                        s.add("scalar", lambda e, dsto=dsto, fc=fc, rb=rb, th=th, ps=pair[th]: e.activation(
                            AB[:, dsto + fc * SEQ + rb * TOK + th * 512: dsto + fc * SEQ + rb * TOK + (th + 1) * 512],
                            ps[:, :], AF.Identity, bias=0.0, scale=1.0))

        s.dma("gpsimd", lambda e: e.dma_start(out=D[0:64, 0:4096], in_=cos_in[:, :]))
        s.dma("gpsimd", lambda e: e.dma_start(out=D[0:64, 4096:8192], in_=sin_in[:, :]))
        cosL = D[0:64, 0:4096]
        sinL = D[0:64, 4096:8192]
        Tlo = AB[0:64, XBO: XBO + SEQ]
        Aq = AB[0:64, XBO + SEQ: XBO + 2 * SEQ]
        Bq = AB[0:64, XBO + 2 * SEQ: XBO + 3 * SEQ]
        C2q = AB[0:64, XBO + 3 * SEQ: XBO + 4 * SEQ]
        for base in (QO, KO):
            for fc in range(4):
                x1 = AB[0:64, base + fc * SEQ: base + (fc + 1) * SEQ]
                x2hi = AB[64:128, base + fc * SEQ: base + (fc + 1) * SEQ]
                s.dma("sync", lambda e, x2hi=x2hi: e.dma_start(out=Tlo, in_=x2hi))
                s.add("vector", lambda e, x1=x1: e.tensor_mul(Aq, x1, cosL))
                s.add("vector", lambda e: e.tensor_mul(C2q, Tlo, sinL))
                s.add("vector", lambda e: e.tensor_sub(Aq, Aq, C2q))
                s.add("vector", lambda e, x1=x1: e.tensor_mul(Bq, x1, sinL))
                s.add("vector", lambda e: e.tensor_mul(C2q, Tlo, cosL))
                s.add("vector", lambda e: e.tensor_add(Bq, Bq, C2q))
                s.add("vector", lambda e, x1=x1: e.tensor_copy(x1, Aq))
                s.dma("sync", lambda e, x2hi=x2hi: e.dma_start(out=x2hi, in_=Bq))
        for fc in range(4):
            for kt in range(32):
                s.add("tensor", lambda e, fc=fc, kt=kt: e.transpose(
                    pT[:, 0:128], AB[:, VO + fc * SEQ + kt * 128: VO + fc * SEQ + (kt + 1) * 128], idb[:, :]))
                s.add("vector", lambda e, fc=fc, kt=kt: e.tensor_copy(
                    AB[:, XBO + kt * 512 + fc * 128: XBO + kt * 512 + (fc + 1) * 128], pT[:, 0:128]))
        VTO = XBO

        for fc in range(4):
            for qc in range(8):
                qsl = slice(qc * 512, (qc + 1) * 512)
                nkt = 4 * qc + 4
                pvac = pC if (fc * 8 + qc) % 2 == 0 else pD
                lrow = pR2[0:1, :] if (fc * 8 + qc) % 2 == 0 else pR2[32:33, :]
                def emit_pv1(kt, fc=fc, nkt=nkt, pvac=pvac, lrow=lrow):
                    pr = (kt // 2) % 2
                    pts = D[:, 8192 + pr * 1024 + (kt % 2) * 512: 8192 + pr * 1024 + (kt % 2 + 1) * 512]
                    s.add("tensor", lambda e, kt=kt, fc=fc, pts=pts, nkt=nkt: e.matmul(
                        pvac[:, :], AB[:, VTO + kt * 512 + fc * 128: VTO + kt * 512 + (fc + 1) * 128], pts,
                        start=(kt == 0), stop=(kt == nkt - 1)))
                    s.add("tensor", lambda e, kt=kt, pts=pts, nkt=nkt: e.matmul(
                        lrow, onb[:, :], pts, start=(kt == 0), stop=(kt == nkt - 1)))
                for kt in range(nkt):
                    ps = pAB[:, (kt % 2) * 512:(kt % 2 + 1) * 512]
                    s.add("tensor", lambda e, fc=fc, kt=kt, qsl=qsl, ps=ps: e.matmul(
                        ps, AB[:, KO + fc * SEQ + kt * 128: KO + fc * SEQ + (kt + 1) * 128],
                        AB[:, QO + fc * SEQ: QO + (fc + 1) * SEQ][:, qsl], start=True, stop=True))
                    j = kt - 4 * qc
                    if j >= 0:
                        s.add("vector", lambda e, ps=ps, j=j: e.tensor_add(ps, ps, msk[:, j * 512:(j + 1) * 512]))
                    if kt % 2 == 1:
                        pr = (kt // 2) % 2
                        ptsp = D[:, 8192 + pr * 1024: 8192 + (pr + 1) * 1024]
                        s.add("scalar", lambda e, ptsp=ptsp: e.activation(ptsp, pAB[:, 0:1024], AF.Exp, bias=shc[:, 0:1], scale=C_ATT))
                        if kt > 1:
                            emit_pv1(kt - 3)
                            emit_pv1(kt - 2)
                emit_pv1(nkt - 2)
                emit_pv1(nkt - 1)
                tslot = E[0:1, 0:512] if (fc * 8 + qc) % 2 == 0 else E[0:1, 512:1024]
                bslot = slice(5120, 5632) if (fc * 8 + qc) % 2 == 0 else slice(5632, 6144)
                s.add("vector", lambda e, lrow=lrow, tslot=tslot: e.reciprocal(tslot, lrow))
                s.add("tensor", lambda e, tslot=tslot: e.matmul(pR1[:, :], orf[:, :], tslot, start=True, stop=True))
                s.add("scalar", lambda e, bslot=bslot: e.activation(C[:, bslot], pR1[:, :], AF.Identity, bias=0.0, scale=1.0))
                s.add("vector", lambda e, fc=fc, qsl=qsl, pvac=pvac, bslot=bslot: e.tensor_mul(
                    AB[:, VO + fc * SEQ: VO + (fc + 1) * SEQ][:, qsl], pvac[:, :], C[:, bslot]))

        s.dma("sync", lambda e: e.dma_start(out=D[:, 0:8192], in_=l1w_sc.ap()[3]))
        for tb in range(4):
            for oc in range(NDC):
                pair = [(pD, pF), (pA, pB)][oc % 2]
                for th in range(2):
                    ps = pair[th]
                    for ic in range(4):
                        s.add("tensor", lambda e, ic=ic, oc=oc, tb=tb, th=th, ps=ps: e.matmul(
                            ps[:, :], D[:, ic * 2048 + oc * 128: ic * 2048 + (oc + 1) * 128],
                            AB[:, VO + ic * SEQ + tb * TOK + th * 512: VO + ic * SEQ + tb * TOK + (th + 1) * 512],
                            start=(ic == 0), stop=(ic == 3)))
                db8 = 8192 if oc % 2 == 0 else 9216
                for th in range(2):
                    s.add("scalar", lambda e, th=th, ps=pair[th], db8=db8: e.activation(
                        D[:, db8 + th * 512:db8 + (th + 1) * 512], ps[:, :], AF.Identity, bias=0.0, scale=1.0))
                s.dma("sync", lambda e, tb=tb, oc=oc, db8=db8: e.dma_start(
                    out=rs_in.ap()[tb * D_MODEL + oc * 128: tb * D_MODEL + (oc + 1) * 128, :], in_=D[:, db8:db8 + 1024]))
        s.add("gpsimd", lambda e: e.collective_compute(
            "ReduceScatter", ALU.add, replica_groups=GROUPS,
            ins=[rs_in.ap()], outs=[rs_out.ap()]), inc=1)
        ln_stats_dram_bf(rs_out)
        ln_residual_bf2d(rs_out)
        if debug_tap == "x3":
            s.dma("sync", lambda e: e.dma_start(out=dbg[:, :, :], in_=x_dram.ap()[:, :, :]))
            s.add("gpsimd", lambda e: e.engine_nop())
            s.emit(nc, st)
            return nc

        XH2 = 32768

        def xh2_ap(dc):
            return AB[:, XH2 + dc * TOK: XH2 + (dc + 1) * TOK]

        ln_stats_dram_f32(x_dram)
        ln_apply(xh2_ap, x_dram)


        for oh in range(2):
            for ocg in range(32):
                og = oh * 32 + ocg
                gb = 0 if ocg % 2 == 0 else 5120
                ub = 2048 if ocg % 2 == 0 else 7168
                s.dma("sync", lambda e, og=og, gb=gb: e.dma_start(
                    out=D[:, gb:gb + 2048], in_=g_sl.ap()[og * 128:(og + 1) * 128, :]))
                s.dma("gpsimd", lambda e, og=og, ub=ub: e.dma_start(
                    out=D[:, ub:ub + 2048], in_=u_sl.ap()[og * 128:(og + 1) * 128, :]))
                gpair = [(pD, pF), (pC, pR1)][ocg % 2]
                for th in range(2):
                    psg = gpair[th]
                    psu = pA if th == 0 else pB
                    for dc in range(NDC):
                        s.add("tensor", lambda e, dc=dc, th=th, psg=psg, gb=gb: e.matmul(
                            psg[:, :], D[:, gb + dc * 128: gb + (dc + 1) * 128], xh2_ap(dc)[:, th * 512:(th + 1) * 512],
                            start=(dc == 0), stop=(dc == NDC - 1)))
                        s.add("tensor", lambda e, dc=dc, th=th, psu=psu, ub=ub: e.matmul(
                            psu[:, :], D[:, ub + dc * 128: ub + (dc + 1) * 128], xh2_ap(dc)[:, th * 512:(th + 1) * 512],
                            start=(dc == 0), stop=(dc == NDC - 1)))
                for th in range(2):
                    s.add("scalar", lambda e, th=th, psg=gpair[th]: e.activation(
                        D[:, 4096 + th * 512:4608 + th * 512], psg[:, :], AF.Silu, bias=0.0, scale=1.0))
                s.add("vector", lambda e, ocg=ocg: e.tensor_mul(AB[:, ocg * TOK: ocg * TOK + 512], pA[:, :], D[:, 4096:4608]))
                s.add("vector", lambda e, ocg=ocg: e.tensor_mul(AB[:, ocg * TOK + 512: (ocg + 1) * TOK], pB[:, :], D[:, 4608:5120]))
            for dc2 in range(NDC):
                db = 0 if dc2 % 2 == 0 else 4096
                s.dma("sync" if dc2 % 2 == 0 else "gpsimd", lambda e, dc2=dc2, oh=oh, db=db: e.dma_start(
                    out=D[:, db:db + 4096],
                    in_=d_sl.ap()[dc2 * 128:(dc2 + 1) * 128, oh * 4096:(oh + 1) * 4096]))
                dpair = [(pD, pF), (pC, pR1)][dc2 % 2]
                for ocg in range(32):
                    for th in range(2):
                        s.add("tensor", lambda e, ocg=ocg, th=th, db=db, ps=dpair[th]: e.matmul(
                            ps[:, :], D[:, db + ocg * 128: db + (ocg + 1) * 128],
                            AB[:, ocg * TOK + th * 512: ocg * TOK + (th + 1) * 512],
                            start=(ocg == 0), stop=(ocg == 31)))
                for th in range(2):
                    s.add("scalar", lambda e, th=th, ps=dpair[th]: e.activation(
                        C[:, 3072 + th * 512:3584 + th * 512], ps[:, :], AF.Identity, bias=0.0, scale=1.0))
                if oh == 0:
                    s.dma("gpsimd", lambda e, dc2=dc2: e.dma_start(out=ffn_acc.ap()[dc2, :, :], in_=C[:, 3072:4096]))
                else:
                    s.dma("gpsimd", lambda e, dc2=dc2: e.dma_start(out=ffn_acc.ap()[dc2, :, :], in_=C[:, 3072:4096], accum_op=ALU.add))
        ln_stats_dram_f32(ffn_acc)
        ln_residual(ffn_acc, sub_is_f32=True, final=True)

        s.add("gpsimd", lambda e: e.engine_nop())
        s.emit(nc, st)
    return nc


def build(debug_tap=None):
    cap = Sched(capture=True)
    build_program(debug_tap, _sched=cap)
    dg = compute_dep_graph(cap.ops, cap.footprints)
    s2 = Sched(dep_graph=dg)
    return build_program(debug_tap, _sched=s2)


_cache = {}


def prep_inputs(inputs):
    consts = host_constants()
    f32 = np.float32
    asf = lambda a: np.ascontiguousarray(np.asarray(a, dtype=f32))
    x = np.asarray(inputs["L0_attn_x"], f32)
    shared = {}
    for w in "qkvo":
        shared[f"w{w}_codes"] = asf(inputs[f"L0_attn_w{w}_codes"])
        shared[f"w{w}_scales"] = asf(np.repeat(np.asarray(inputs[f"L0_attn_w{w}_scales"]), 32, axis=0))
    bias = np.stack([inputs["L0_attn_bq"], inputs["L0_attn_bk"], inputs["L0_attn_bv"],
                     inputs["L0_attn_bo"]], axis=0).astype(f32)
    shared["bias_qkvo"] = asf(bias.reshape(4, NDC, 128).transpose(2, 0, 1).reshape(128, 4 * NDC))
    shared["gateT"] = asf(np.asarray(inputs["L0_ffn_gate"]).T.reshape(NDC, 128, NEXP).transpose(1, 0, 2).reshape(128, NDC * NEXP))
    shared["ebias"] = asf(inputs["L0_ffn_expert_bias"])
    def blk_up(wT):
        return np.ascontiguousarray(wT.reshape(16, 128, 8, 128).transpose(2, 1, 0, 3).reshape(8, 128, 2048))

    def blk_dn(wT):
        return np.ascontiguousarray(wT.reshape(8, 128, 16, 128).transpose(2, 1, 0, 3).reshape(16, 128, 1024))
    w1T_ = np.asarray(inputs["L0_ffn_w1"]).transpose(0, 2, 1).astype(f32)
    w3T_ = np.asarray(inputs["L0_ffn_w3"]).transpose(0, 2, 1).astype(f32)
    w2T_ = np.asarray(inputs["L0_ffn_w2"]).transpose(0, 2, 1).astype(f32)
    shared["w1R"] = np.stack([blk_up(w1T_[e]) for e in range(8)])
    shared["w3R"] = np.stack([blk_up(w3T_[e]) for e in range(8)])
    shared["w2R"] = np.stack([blk_dn(w2T_[e]) for e in range(8)])
    shared["sw1R"] = blk_up(np.asarray(inputs["L0_ffn_shared_w1"]).T.astype(f32))
    shared["sw3R"] = blk_up(np.asarray(inputs["L0_ffn_shared_w3"]).T.astype(f32))
    shared["sw2R"] = blk_dn(np.asarray(inputs["L0_ffn_shared_w2"]).T.astype(f32))
    shared.update(consts)
    s0_gu = spline_cols("gu")
    s0_d = spline_cols("d")
    spl_all = {}
    for (wk, cp, s0) in (("g", inputs["L1_ffn_gate_cp"], s0_gu),
                         ("u", inputs["L1_ffn_up_cp"], s0_gu),
                         ("d", inputs["L1_ffn_down_cp"], s0_d)):
        Q, PH = spline_host_rows(np.asarray(cp, f32), s0)
        spl_all[wk] = (Q, PH)
    maps = []
    for c in range(8):
        b, r = c // 4, c % 4
        m = dict(shared)
        xs = x[b, r * TOK:(r + 1) * TOK, :]
        m["x0"] = asf(xs.T.reshape(NDC, 128, TOK))
        m["l1q_nat"] = asf(np.asarray(inputs["L1_attn_w_q"])[r * 512:(r + 1) * 512, :])
        m["l1k_nat"] = asf(np.asarray(inputs["L1_attn_w_k"])[r * 512:(r + 1) * 512, :])
        m["l1v_nat"] = asf(np.asarray(inputs["L1_attn_w_v"])[r * 512:(r + 1) * 512, :])
        m["l1o_nat"] = asf(np.asarray(inputs["L1_attn_w_o"])[:, r * 512:(r + 1) * 512])
        for wk in ("g", "u", "d"):
            Q, PH = spl_all[wk]
            csl = slice(c * CPC, (c + 1) * CPC)
            m[f"spl_{wk}_Q"] = np.ascontiguousarray(Q[:, csl])
            m[f"spl_{wk}_phi"] = np.ascontiguousarray(PH[:, csl])
        maps.append(m)
    return maps


def kernel(**inputs):
    if "nc" not in _cache:
        _cache["nc"] = build()
    nc = _cache["nc"]
    maps = prep_inputs(inputs)
    res = run_bass_kernel_spmd(nc, maps, core_ids=list(range(8)))
    out = np.zeros((2, SEQ, D_MODEL), np.float32)
    for c in range(8):
        b, r = c // 4, c % 4
        oc = res.results[c]["out"]
        out[b, r * TOK:(r + 1) * TOK, :] = oc.reshape(D_MODEL, TOK).T
    return out


# version 21
# speedup vs baseline: 1.0278x; 1.0278x over previous
import contextlib
import math
import numpy as np
import ml_dtypes

import concourse.bass as bass
import concourse.mybir as mybir
from concourse.bass_utils import run_bass_kernel_spmd

F32 = mybir.dt.float32
BF16 = mybir.dt.bfloat16
AF = mybir.ActivationFunctionType
ALU = mybir.AluOpType
AX = mybir.AxisListType
BFD = ml_dtypes.bfloat16

D_MODEL = 2048
NDC = 16
TOK = 1024
SEQ = 4096
NH = 16
C_ATT = 1.0 / math.sqrt(128.0)
SHIFT = 12.0
NEXP = 8
INTER = 8192
NCTRL = 130156
NSAMP = INTER * D_MODEL
RSTEP = (NCTRL - 1.0) / (NSAMP - 1.0)
NCOL = NSAMP // 128
CPC = NCOL // 8
LN_EPS = 1e-5
TERN_EPS = 1e-8
GROUPS = [[0, 1, 2, 3], [4, 5, 6, 7]]

ENG_NAMES = ("sync", "tensor", "vector", "scalar", "gpsimd")


class Sched:

    NSEM_DMA = 8

    def __init__(self, capture=False, dep_graph=None):
        self.ops = []
        self.cum = {e: 0 for e in ENG_NAMES}
        self.ndma = {e: 0 for e in ENG_NAMES}
        self.ncc = 0
        self.capture = capture
        self.dep_graph = dep_graph
        self.footprints = []

    def add(self, eng, fn, deps=None, inc=None, dma=False):
        assert eng in ENG_NAMES
        idx = len(self.ops)
        kind = "op"
        if inc == 1 and eng == "gpsimd" and not dma:
            pass
        if dma or eng == "sync":
            kind = "dma"
        if inc == 1 and not dma and eng == "gpsimd" and fn.__code__.co_names and "collective_compute" in fn.__code__.co_names:
            kind = "cc"
        if kind == "dma":
            n = self.ndma[eng]
            self.ndma[eng] += 1
            self.ops.append(dict(eng=eng, fn=fn, mdeps=list(deps) if deps else [],
                                 kind="dma", n=n))
        elif kind == "cc":
            self.ncc += 1
            self.ops.append(dict(eng=eng, fn=fn, mdeps=list(deps) if deps else [],
                                 kind="cc", n=self.ncc))
        else:
            self.cum[eng] += 1
            self.ops.append(dict(eng=eng, fn=fn, mdeps=list(deps) if deps else [],
                                 kind="op", n=self.cum[eng]))
        return idx

    def dma(self, eng, fn, deps=None):
        return self.add(eng, fn, deps=deps, dma=True)

    SBUF_NAMES = frozenset(("AB", "Cc", "Dd", "Ee", "idb", "idf", "onb", "onf", "orf",
                            "msk", "biasq", "gtb", "ebc", "b4s", "b2s", "rr4", "rr2",
                            "shc", "slb"))
    PSUM_NAMES = frozenset(("pAB", "pC", "pD", "pF", "pR1", "pR2", "pT"))

    @classmethod
    def _footprint(cls, mi):
        reads, writes = [], []
        for lst, acc in ((mi.ins, reads), (mi.outs, writes)):
            for a in lst:
                try:
                    bap = a.bass_ap
                    if bap is None:
                        continue
                    name = bap.tensor.name
                    shape = list(bap.tensor.shape)
                    off = bap.offset
                    ap = list(bap.ap)
                    if name in cls.PSUM_NAMES:
                        acc.append((name, 0, 1 << 40))
                    elif name in cls.SBUF_NAMES:
                        rowlen = shape[-1]
                        col0 = off % rowlen
                        span = 0
                        for (step, count) in ap:
                            if step == rowlen and count <= 128:
                                continue
                            span += (count - 1) * step
                        acc.append((name, col0, col0 + span + 1))
                    else:
                        span = 0
                        for (step, count) in ap:
                            span += (count - 1) * step
                        acc.append((name, off, off + span + 1))
                except Exception:
                    acc.append(("__unknown__", 0, 1 << 40))
        return reads, writes

    def emit(self, nc, stack):
        csem = {e: stack.enter_context(nc.semaphore(f"sq_{e}")) for e in ENG_NAMES}
        dsem = {e: [stack.enter_context(nc.semaphore(f"sd_{e}{j}")) for j in range(self.NSEM_DMA)]
                for e in ("sync", "gpsimd", "scalar", "vector")}
        ccsem = stack.enter_context(nc.semaphore("sq_cc"))
        block = stack.enter_context(nc.Block())
        ops = self.ops
        cap = self.capture
        dg = self.dep_graph
        fps = self.footprints
        K = self.NSEM_DMA

        def need_wait(d, i):
            return not (ops[d]["eng"] == "tensor" and ops[i]["eng"] == "tensor"
                        and ops[d]["kind"] == "op" and ops[i]["kind"] == "op")

        tinc = [False] * len(ops)
        tval = [0] * len(ops)
        if not cap and dg is not None:
            for i, o in enumerate(ops):
                for d in dg[i] + o["mdeps"]:
                    if d >= 0 and need_wait(d, i) and ops[d]["eng"] == "tensor" and ops[d]["kind"] == "op":
                        tinc[d] = True
            run = 0
            for i, o in enumerate(ops):
                if o["eng"] == "tensor" and o["kind"] == "op":
                    if tinc[i]:
                        run += 1
                    tval[i] = run

        def wait_for(eng, water, d):
            o = ops[d]
            if o["kind"] == "dma":
                key = (o["eng"], o["n"] % K)
                val = 16 * (o["n"] // K + 1)
                sem = dsem[o["eng"]][o["n"] % K]
            elif o["kind"] == "cc":
                key = ("cc", 0)
                val = o["n"]
                sem = ccsem
            else:
                key = (o["eng"], -1)
                val = tval[d] if o["eng"] == "tensor" else o["n"]
                sem = csem[o["eng"]]
            if water.get(key, 0) < val:
                eng.wait_ge(sem, val)
                water[key] = val

        def runner(ename):
            def body(eng):
                water = {}
                for i, o in enumerate(ops):
                    if o["eng"] != ename:
                        continue
                    if not cap:
                        for d in (dg[i] if dg is not None else ([i - 1] if i > 0 else [])):
                            if d >= 0 and need_wait(d, i):
                                wait_for(eng, water, d)
                        for d in o["mdeps"]:
                            if d >= 0 and need_wait(d, i):
                                wait_for(eng, water, d)
                        if o["kind"] == "dma":
                            j = o["n"] % K
                            prevval = 16 * (o["n"] // K)
                            if prevval > 0 and water.get((ename, j), 0) < prevval:
                                eng.wait_ge(dsem[ename][j], prevval)
                                water[(ename, j)] = prevval
                    inst = o["fn"](eng)
                    if o["kind"] == "dma":
                        inst.then_inc(dsem[ename][o["n"] % K], 16)
                    elif o["kind"] == "cc":
                        inst.then_inc(ccsem, 1)
                    elif ename == "tensor" and not cap:
                        if tinc[i]:
                            inst.then_inc(csem[ename], 1)
                    else:
                        inst.then_inc(csem[ename], 1)
                    if cap:
                        while len(fps) <= i:
                            fps.append(None)
                        fps[i] = self._footprint(inst.ins)
            return body

        block.sync(runner("sync"))
        block.tensor(runner("tensor"))
        block.vector(runner("vector"))
        block.scalar(runner("scalar"))
        block.gpsimd(runner("gpsimd"))


def compute_dep_graph(ops, fps):
    writes = {}
    reads = {}
    deps = [[] for _ in ops]
    out_writers = []
    for i in range(len(ops)):
        fp = fps[i] if i < len(fps) and fps[i] is not None else ([("__unknown__", 0, 1 << 40)], [("__unknown__", 0, 1 << 40)])
        rd, wr = fp
        dd = set()
        for (name, lo, hi) in rd:
            for rec in writes.get(name, ()):
                if rec[0] < hi and lo < rec[1]:
                    dd.add(rec[2])
            reads.setdefault(name, []).append([lo, hi, i])
        for (name, lo, hi) in wr:
            wl = writes.setdefault(name, [])
            rl = reads.setdefault(name, [])
            keepw = []
            for rec in wl:
                if rec[0] < hi and lo < rec[1]:
                    dd.add(rec[2])
                    if not (lo <= rec[0] and rec[1] <= hi):
                        keepw.append(rec)
                else:
                    keepw.append(rec)
            keepw.append([lo, hi, i])
            writes[name] = keepw
            keepr = []
            for rec in rl:
                if rec[0] < hi and lo < rec[1]:
                    dd.add(rec[2])
                    if not (lo <= rec[0] and rec[1] <= hi):
                        keepr.append(rec)
                else:
                    keepr.append(rec)
            reads[name] = keepr
            if name in ("out", "dbg"):
                out_writers.append(i)
        dd.discard(i)
        deps[i] = sorted(dd)
    if out_writers:
        deps[-1] = sorted(set(deps[-1]) | set(out_writers))
    return deps


def host_constants():
    c = {}
    c["ident_bf"] = np.eye(128, dtype=BFD)
    c["ident_f"] = np.eye(128, dtype=np.float32)
    c["ones_bf"] = np.ones((128, 1), dtype=BFD)
    c["ones_f"] = np.ones((128, 1), dtype=np.float32)
    c["ones_row_f"] = np.ones((1, 128), dtype=np.float32)
    masks = np.zeros((4, 128, 512), dtype=np.float32)
    for j in range(4):
        for p in range(128):
            masks[j, p, :j * 128 + p] = -1e30
    c["masks"] = np.ascontiguousarray(masks.transpose(1, 0, 2).reshape(128, 2048))
    half = 64
    inv = 1.0 / (10000.0 ** (np.arange(half, dtype=np.float32) / half))
    fr = np.outer(np.arange(SEQ, dtype=np.float32), inv)
    c["cos_f"] = np.ascontiguousarray(np.cos(fr).T).astype(np.float32)
    c["sin_f"] = np.ascontiguousarray(np.sin(fr).T).astype(np.float32)
    pR = (np.arange(128, dtype=np.float64) * RSTEP).astype(np.float32)
    c["spl_b4"] = np.stack([np.ones(128, np.float32), pR, pR ** 2, pR ** 3], axis=0).astype(np.float32)
    c["spl_b2"] = np.stack([np.ones(128, np.float32), pR], axis=0).astype(np.float32)
    selb = np.zeros((8, 8 * 128), dtype=np.float32)
    for e in range(8):
        selb[e, e * 128:(e + 1) * 128] = 1.0
    c["selb"] = selb
    return c


def spline_cols(kind):
    if kind == "gu":
        og = np.arange(64, dtype=np.int64)[:, None, None]
        dc = np.arange(16, dtype=np.int64)[None, :, None]
        j = np.arange(128, dtype=np.int64)[None, None, :]
        s0 = ((og * 128 + j) * D_MODEL + dc * 128).reshape(-1)
    else:
        dc2 = np.arange(16, dtype=np.int64)[:, None, None]
        hc = np.arange(64, dtype=np.int64)[None, :, None]
        j = np.arange(128, dtype=np.int64)[None, None, :]
        s0 = ((dc2 * 128 + j) * INTER + hc * 128).reshape(-1)
    return s0


def spline_host_rows(cp, s0):
    t = s0.astype(np.float64) * RSTEP
    I = np.clip(np.floor(t).astype(np.int64), 0, NCTRL - 2)
    phi = (t - I).astype(np.float32)
    Q = np.stack([cp[np.clip(I - 1 + k, 0, NCTRL - 1)] for k in range(5)], axis=0)
    return Q.astype(np.float32), np.stack([phi, phi * phi, phi * phi * phi], axis=0).astype(np.float32)


def build_program(debug_tap=None, _sched=None):
    nc = bass.Bass("TRN2", target_bir_lowering=False, debug=False)

    def din(name, shape, dt=F32):
        return nc.dram_tensor(name, list(shape), dt, kind="ExternalInput").ap()

    def dint(name, shape, dt=F32):
        return nc.dram_tensor(name, list(shape), dt, kind="Internal")

    x0_in = din("x0", [NDC, 128, TOK])
    l0c = {w: din(f"w{w}_codes", [D_MODEL, D_MODEL]) for w in "qkvo"}
    l0s = {w: din(f"w{w}_scales", [D_MODEL, D_MODEL]) for w in "qkvo"}
    bias_qkvo = din("bias_qkvo", [128, 4 * NDC])
    gateT_in = din("gateT", [128, NDC * NEXP])
    ebias_in = din("ebias", [NEXP])
    w1R = din("w1R", [NEXP, 8, 128, 2048])
    w3R = din("w3R", [NEXP, 8, 128, 2048])
    w2R = din("w2R", [NEXP, 16, 128, 1024])
    sw1R = din("sw1R", [8, 128, 2048])
    sw3R = din("sw3R", [8, 128, 2048])
    sw2R = din("sw2R", [16, 128, 1024])
    l1q_nat = din("l1q_nat", [512, D_MODEL])
    l1k_nat = din("l1k_nat", [512, D_MODEL])
    l1v_nat = din("l1v_nat", [512, D_MODEL])
    l1o_nat = din("l1o_nat", [D_MODEL, 512])
    spl = {}
    for w in ("g", "u", "d"):
        spl[w + "_Q"] = din(f"spl_{w}_Q", [5, CPC])
        spl[w + "_phi"] = din(f"spl_{w}_phi", [3, CPC])
    ident_bf = din("ident_bf", [128, 128], BF16)
    ident_f = din("ident_f", [128, 128])
    ones_bf = din("ones_bf", [128, 1], BF16)
    ones_f = din("ones_f", [128, 1])
    ones_row_f = din("ones_row_f", [1, 128])
    masks_in = din("masks", [128, 2048])
    cos_in = din("cos_f", [64, SEQ])
    sin_in = din("sin_f", [64, SEQ])
    b4_in = din("spl_b4", [4, 128])
    b2_in = din("spl_b2", [2, 128])
    selb_in = din("selb", [8, 1024])

    out_final = nc.dram_tensor("out", [NDC, 128, TOK], F32, kind="ExternalOutput").ap()
    dbg = nc.dram_tensor("dbg", [NDC, 128, TOK], F32, kind="ExternalOutput").ap() if debug_tap else None

    x_dram = dint("x_dram", [NDC, 128, TOK])
    sub_dram = dint("sub_dram", [NDC, 128, TOK], BF16)
    k_ag_in = dint("k_ag_in", [D_MODEL, TOK], BF16)
    v_ag_in = dint("v_ag_in", [D_MODEL, TOK], BF16)
    k_ag_out = dint("k_ag_out", [4 * D_MODEL, TOK], BF16)
    v_ag_out = dint("v_ag_out", [4 * D_MODEL, TOK], BF16)
    xh_ag_in = dint("xh_ag_in", [D_MODEL, TOK], BF16)
    xh_ag_out = dint("xh_ag_out", [4 * D_MODEL, TOK], BF16)
    rs_in = dint("rs_in", [4 * D_MODEL, TOK], BF16)
    rs_out = dint("rs_out", [D_MODEL, TOK], BF16)
    g_sl_in = dint("g_sl_in", [8 * 128, D_MODEL], BF16)
    u_sl_in = dint("u_sl_in", [8 * 128, D_MODEL], BF16)
    d_sl_in = dint("d_sl_in", [2 * 128, INTER], BF16)
    g_sl = dint("g_sl", [64 * 128, D_MODEL], BF16)
    u_sl = dint("u_sl", [64 * 128, D_MODEL], BF16)
    d_sl = dint("d_sl", [16 * 128, INTER], BF16)
    w1B = dint("w1B", [NEXP, 8, 128, 2048], BF16)
    w3B = dint("w3B", [NEXP, 8, 128, 2048], BF16)
    w2B = dint("w2B", [NEXP, 16, 128, 1024], BF16)
    sw1B = dint("sw1B", [8, 128, 2048], BF16)
    sw3B = dint("sw3B", [8, 128, 2048], BF16)
    sw2B = dint("sw2B", [16, 128, 1024], BF16)
    rows_dram = dint("rows_dram", [8, CPC])
    ffn_acc = dint("ffn_acc", [NDC, 128, TOK])
    l1w_sc = dint("l1w_sc", [4, 128, 8192], BF16)

    st = contextlib.ExitStack()
    with st:
        AB = st.enter_context(nc.sbuf_tensor("AB", [128, 65536], BF16))
        C = st.enter_context(nc.sbuf_tensor("Cc", [128, 6144], F32))
        D = st.enter_context(nc.sbuf_tensor("Dd", [128, 12288], BF16))
        E = st.enter_context(nc.sbuf_tensor("Ee", [128, 2048], F32))
        idb = st.enter_context(nc.sbuf_tensor("idb", [128, 128], BF16))
        idf = st.enter_context(nc.sbuf_tensor("idf", [128, 128], F32))
        onb = st.enter_context(nc.sbuf_tensor("onb", [128, 1], BF16))
        onf = st.enter_context(nc.sbuf_tensor("onf", [128, 1], F32))
        orf = st.enter_context(nc.sbuf_tensor("orf", [1, 128], F32))
        msk = st.enter_context(nc.sbuf_tensor("msk", [128, 2048], BF16))
        biasq = st.enter_context(nc.sbuf_tensor("biasq", [128, 4 * NDC], F32))
        gtb = st.enter_context(nc.sbuf_tensor("gtb", [128, NDC * NEXP], BF16))
        ebc = st.enter_context(nc.sbuf_tensor("ebc", [NEXP, 1], F32))
        b4s = st.enter_context(nc.sbuf_tensor("b4s", [4, 128], F32))
        b2s = st.enter_context(nc.sbuf_tensor("b2s", [2, 128], F32))
        rr4 = st.enter_context(nc.sbuf_tensor("rr4", [4, 1024], F32))
        rr2 = st.enter_context(nc.sbuf_tensor("rr2", [2, 2048], F32))
        shc = st.enter_context(nc.sbuf_tensor("shc", [128, 1], F32))
        slb = st.enter_context(nc.sbuf_tensor("slb", [8, 1024], F32))
        pAB = st.enter_context(nc.psum_tensor("pAB", [128, 1024], F32))
        pA = pAB[:, 0:512]
        pB = pAB[:, 512:1024]
        pC = st.enter_context(nc.psum_tensor("pC", [128, 512], F32))
        pD = st.enter_context(nc.psum_tensor("pD", [128, 512], F32))
        pF = st.enter_context(nc.psum_tensor("pF", [128, 512], F32))
        pR1 = st.enter_context(nc.psum_tensor("pR1", [128, 512], F32))
        pR2 = st.enter_context(nc.psum_tensor("pR2", [128, 512], F32))
        pT = st.enter_context(nc.psum_tensor("pT", [128, 1024], BF16))

        s = _sched if _sched is not None else Sched(capture=True)

        MEAN = E[0:1, 0:1024]
        RSTD = E[0:1, 1024:2048]
        T1 = rr2[0:1, 1024:2048]
        T2 = C[0:1, 5120:6144]

        s.dma("sync", lambda e: e.dma_start(out=idb[:, :], in_=ident_bf[:, :]))
        s.dma("sync", lambda e: e.dma_start(out=idf[:, :], in_=ident_f[:, :]))
        s.dma("sync", lambda e: e.dma_start(out=onb[:, :], in_=ones_bf[:, :]))
        s.dma("sync", lambda e: e.dma_start(out=onf[:, :], in_=ones_f[:, :]))
        s.dma("sync", lambda e: e.dma_start(out=orf[:, :], in_=ones_row_f[:, :]))
        s.dma("gpsimd", lambda e: e.dma_start(out=msk[:, :], in_=masks_in[:, :]))
        s.dma("sync", lambda e: e.dma_start(out=biasq[:, :], in_=bias_qkvo[:, :]))
        s.dma("gpsimd", lambda e: e.dma_start(out=gtb[:, :], in_=gateT_in[:, :]))
        s.dma("sync", lambda e: e.dma_start(out=ebc[:, 0], in_=ebias_in[:]))
        s.dma("sync", lambda e: e.dma_start(out=b4s[:, :], in_=b4_in[:, :]))
        s.dma("sync", lambda e: e.dma_start(out=b2s[:, :], in_=b2_in[:, :]))
        s.dma("sync", lambda e: e.dma_start(out=x_dram.ap(), in_=x0_in[:, :, :]))
        s.add("vector", lambda e: e.memset(shc[:, :], -SHIFT))
        s.dma("sync", lambda e: e.dma_start(out=slb[:, :], in_=selb_in[:, :]))

        def stats_mm(chunk_f32, sq_f32, first, last):
            for th in range(2):
                sl = slice(th * 512, (th + 1) * 512)
                rS = pR1[0:1, :] if th == 0 else pR1[32:33, :]
                rQ = pR2[0:1, :] if th == 0 else pR2[32:33, :]
                s.add("tensor", lambda e, sl=sl, rS=rS, ch=chunk_f32, first=first, last=last: e.matmul(
                    rS, onf[:, :], ch[:, sl], start=first, stop=last))
                s.add("tensor", lambda e, sl=sl, rQ=rQ, sq=sq_f32, first=first, last=last: e.matmul(
                    rQ, onf[:, :], sq[:, sl], start=first, stop=last))

        def ln_rows():
            for th in range(2):
                sl = slice(th * 512, (th + 1) * 512)
                rS = pR1[0:1, :] if th == 0 else pR1[32:33, :]
                rQ = pR2[0:1, :] if th == 0 else pR2[32:33, :]
                s.add("vector", lambda e, sl=sl, rS=rS: e.tensor_scalar(MEAN[:, sl], rS, 1.0 / 2048.0, None, ALU.mult))
                s.add("vector", lambda e, sl=sl, rQ=rQ: e.tensor_scalar(T1[:, sl], rQ, 1.0 / 2048.0, None, ALU.mult))
            s.add("vector", lambda e: e.tensor_mul(T2[:, :], MEAN[:, :], MEAN[:, :]))
            s.add("vector", lambda e: e.tensor_sub(T1[:, :], T1[:, :], T2[:, :]))
            s.add("vector", lambda e: e.tensor_scalar(T1[:, :], T1[:, :], LN_EPS, None, ALU.add))
            s.add("scalar", lambda e: e.activation(T2[:, :], T1[:, :], AF.Ln, bias=0.0, scale=1.0))
            s.add("scalar", lambda e: e.activation(RSTD[:, :], T2[:, :], AF.Exp, bias=0.0, scale=-0.5))

        def ln_stats_dram_f32(src):
            for dc in range(NDC):
                cb = 3072 if dc % 2 == 0 else 4096
                qb = 0 if dc % 2 == 0 else 1024
                s.dma("sync", lambda e, dc=dc, cb=cb: e.dma_start(out=C[:, cb:cb + 1024], in_=src.ap()[dc, :, :]))
                eng_q = "gpsimd" if dc % 2 == 0 else "vector"
                s.add(eng_q, lambda e, cb=cb, qb=qb: e.tensor_mul(C[:, qb:qb + 1024], C[:, cb:cb + 1024], C[:, cb:cb + 1024]))
                stats_mm(C[:, cb:cb + 1024], C[:, qb:qb + 1024], dc == 0, dc == NDC - 1)
            ln_rows()

        def ln_bcast_ab():
            s.add("vector", lambda e: e.tensor_mul(T2[:, :], MEAN[:, :], RSTD[:, :]))
            for th in range(2):
                sl = slice(th * 512, (th + 1) * 512)
                s.add("tensor", lambda e, sl=sl: e.matmul(pR1[:, :], orf[:, :], RSTD[:, sl], start=True, stop=True))
                s.add("scalar", lambda e, sl=sl: e.activation(C[:, 0:1024][:, sl], pR1[:, :], AF.Identity, bias=0.0, scale=1.0))
                s.add("tensor", lambda e, sl=sl: e.matmul(pR1[:, :], orf[:, :], T2[:, sl], start=True, stop=True))
                s.add("scalar", lambda e, sl=sl: e.activation(C[:, 1024:2048][:, sl], pR1[:, :], AF.Identity, bias=0.0, scale=-1.0))

        def ln_apply(dst_fn, src):
            ln_bcast_ab()
            for dc in range(NDC):
                cb = 3072 if dc % 2 == 0 else 4096
                tb = 5120 if dc % 2 == 0 else 2048
                s.dma("sync", lambda e, dc=dc, cb=cb: e.dma_start(out=C[:, cb:cb + 1024], in_=src.ap()[dc, :, :]))
                eng_a = "vector" if dc % 2 == 0 else "gpsimd"
                s.add(eng_a, lambda e, cb=cb, tb=tb: e.tensor_mul(C[:, tb:tb + 1024], C[:, cb:cb + 1024], C[:, 0:1024]))
                s.add(eng_a, lambda e, dst=dst_fn(dc), tb=tb: e.tensor_add(dst, C[:, tb:tb + 1024], C[:, 1024:2048]))

        def ln_residual(sub_src, sub_is_f32=False, final=False):
            ln_bcast_ab()
            for dc in range(NDC):
                xb = 3072 if dc % 2 == 0 else 4096
                tb = 5120 if dc % 2 == 0 else 2048
                if sub_is_f32:
                    sb_c = 2048 if dc % 2 == 0 else 5120
                    s.dma("sync", lambda e, dc=dc, sb_c=sb_c: e.dma_start(out=C[:, sb_c:sb_c + 1024], in_=sub_src.ap()[dc, :, :]))
                    subchunk = C[:, sb_c:sb_c + 1024]
                else:
                    db_c = 10240 if dc % 2 == 0 else 11264
                    s.dma("sync", lambda e, dc=dc, db_c=db_c: e.dma_start(out=D[:, db_c:db_c + 1024], in_=sub_src.ap()[dc, :, :]))
                    subchunk = D[:, db_c:db_c + 1024]
                s.dma("sync", lambda e, dc=dc, xb=xb: e.dma_start(out=C[:, xb:xb + 1024], in_=x_dram.ap()[dc, :, :]))
                eng_r = "vector" if dc % 2 == 0 else "gpsimd"
                if sub_is_f32:
                    s.add(eng_r, lambda e, sc=subchunk: e.tensor_mul(sc, sc, C[:, 0:1024]))
                    s.add(eng_r, lambda e, sc=subchunk: e.tensor_add(sc, sc, C[:, 1024:2048]))
                    s.add(eng_r, lambda e, xb=xb, sc=subchunk: e.tensor_add(C[:, xb:xb + 1024], C[:, xb:xb + 1024], sc))
                else:
                    s.add(eng_r, lambda e, sc=subchunk, tb=tb: e.tensor_mul(C[:, tb:tb + 1024], sc, C[:, 0:1024]))
                    s.add(eng_r, lambda e, tb=tb: e.tensor_add(C[:, tb:tb + 1024], C[:, tb:tb + 1024], C[:, 1024:2048]))
                    s.add(eng_r, lambda e, xb=xb, tb=tb: e.tensor_add(C[:, xb:xb + 1024], C[:, xb:xb + 1024], C[:, tb:tb + 1024]))
                tgt = out_final if final else x_dram.ap()
                s.dma("sync", lambda e, dc=dc, tgt=tgt, xb=xb: e.dma_start(out=tgt[dc, :, :], in_=C[:, xb:xb + 1024]))

        def ln_stats_dram_bf(src2d):
            for dc in range(NDC):
                s.dma("sync", lambda e, dc=dc: e.dma_start(out=D[:, 10240:11264], in_=src2d.ap()[dc * 128:(dc + 1) * 128, :]))
                s.add("vector", lambda e: e.tensor_copy(C[:, 3072:4096], D[:, 10240:11264]))
                s.add("gpsimd", lambda e: e.tensor_mul(C[:, 4096:5120], C[:, 3072:4096], C[:, 3072:4096]))
                stats_mm(C[:, 3072:4096], C[:, 4096:5120], dc == 0, dc == NDC - 1)
            ln_rows()

        def ln_residual_bf2d(src2d, final=False):
            ln_bcast_ab()
            for dc in range(NDC):
                s.dma("sync", lambda e, dc=dc: e.dma_start(out=D[:, 10240:11264], in_=src2d.ap()[dc * 128:(dc + 1) * 128, :]))
                s.dma("sync", lambda e, dc=dc: e.dma_start(out=C[:, 3072:4096], in_=x_dram.ap()[dc, :, :]))
                s.add("vector", lambda e: e.tensor_mul(C[:, 5120:6144], D[:, 10240:11264], C[:, 0:1024]))
                s.add("vector", lambda e: e.tensor_add(C[:, 5120:6144], C[:, 5120:6144], C[:, 1024:2048]))
                s.add("gpsimd", lambda e: e.tensor_add(C[:, 3072:4096], C[:, 3072:4096], C[:, 5120:6144]))
                tgt = out_final if final else x_dram.ap()
                s.dma("sync", lambda e, dc=dc, tgt=tgt: e.dma_start(out=tgt[dc, :, :], in_=C[:, 3072:4096]))

        def dump_dbg_bf(ap_fn):
            for dc in range(NDC):
                s.add("vector", lambda e, dc=dc: e.tensor_copy(C[:, 3072:4096], ap_fn(dc)))
                s.dma("sync", lambda e, dc=dc: e.dma_start(out=dbg[dc, :, :], in_=C[:, 3072:4096]))
            s.add("gpsimd", lambda e: e.engine_nop())
            s.emit(nc, st)

        WOFF, XHOFF, QFOFF = 0, 32768, 49152

        def xh_ap(dc):
            return AB[:, XHOFF + dc * TOK: XHOFF + (dc + 1) * TOK]

        ln_stats_dram_f32(x_dram)
        ln_apply(xh_ap, x_dram)
        if debug_tap == "xh1":
            dump_dbg_bf(xh_ap)
            return nc

        def recon_gen(wkey, slab_in, ncol_row):
            Q = spl[wkey + "_Q"]
            PH = spl[wkey + "_phi"]
            for k in range(5):
                s.dma("sync", lambda e, k=k, Q=Q: e.dma_start(
                    out=C[:, k * 128:(k + 1) * 128], in_=Q[k:k + 1, :].rearrange("o (p f) -> (o p) f", p=128)))
            for k in range(3):
                s.dma("sync", lambda e, k=k, PH=PH: e.dma_start(
                    out=C[:, 640 + k * 128: 640 + (k + 1) * 128], in_=PH[k:k + 1, :].rearrange("o (p f) -> (o p) f", p=128)))
            Q0, Q1, Q2, Q3, Q4 = (C[:, k * 128:(k + 1) * 128] for k in range(5))
            P1_, P2_, P3_ = (C[:, 640 + k * 128: 640 + (k + 1) * 128] for k in range(3))

            def rowt(i):
                return C[:, 1024 + i * 128: 1024 + (i + 1) * 128]
            a1, a2, a3, bb2, bb3, ee0, ee1, t0, t1, al0, al1, al2, um1, zz0, on1 = (rowt(i) for i in range(15))
            v = lambda f: s.add("vector", f)
            v(lambda e: e.tensor_sub(a1, Q2, Q0)); v(lambda e: e.tensor_scalar_mul(a1, a1, 0.5))
            v(lambda e: e.scalar_tensor_tensor(t0, Q1, -2.5, Q0, ALU.mult, ALU.add))
            v(lambda e: e.scalar_tensor_tensor(t1, Q2, 2.0, t0, ALU.mult, ALU.add))
            v(lambda e: e.scalar_tensor_tensor(a2, Q3, -0.5, t1, ALU.mult, ALU.add))
            v(lambda e: e.tensor_scalar_mul(t0, Q0, -0.5))
            v(lambda e: e.scalar_tensor_tensor(t1, Q1, 1.5, t0, ALU.mult, ALU.add))
            v(lambda e: e.scalar_tensor_tensor(t0, Q2, -1.5, t1, ALU.mult, ALU.add))
            v(lambda e: e.scalar_tensor_tensor(a3, Q3, 0.5, t0, ALU.mult, ALU.add))
            v(lambda e: e.scalar_tensor_tensor(t0, Q2, -2.5, Q1, ALU.mult, ALU.add))
            v(lambda e: e.scalar_tensor_tensor(t1, Q3, 2.0, t0, ALU.mult, ALU.add))
            v(lambda e: e.scalar_tensor_tensor(bb2, Q4, -0.5, t1, ALU.mult, ALU.add))
            v(lambda e: e.tensor_scalar_mul(t0, Q1, -0.5))
            v(lambda e: e.scalar_tensor_tensor(t1, Q2, 1.5, t0, ALU.mult, ALU.add))
            v(lambda e: e.scalar_tensor_tensor(t0, Q3, -1.5, t1, ALU.mult, ALU.add))
            v(lambda e: e.scalar_tensor_tensor(bb3, Q4, 0.5, t0, ALU.mult, ALU.add))
            v(lambda e: e.tensor_sub(ee1, bb3, a3))
            v(lambda e: e.tensor_sub(t0, bb2, bb3)); v(lambda e: e.tensor_sub(t0, t0, a2))
            v(lambda e: e.scalar_tensor_tensor(ee0, a3, -2.0, t0, ALU.mult, ALU.add))
            v(lambda e: e.tensor_mul(t0, a1, P1_)); v(lambda e: e.tensor_add(al0, Q1, t0))
            v(lambda e: e.tensor_mul(t0, a2, P2_)); v(lambda e: e.tensor_add(al0, al0, t0))
            v(lambda e: e.tensor_mul(t0, a3, P3_)); v(lambda e: e.tensor_add(al0, al0, t0))
            v(lambda e: e.tensor_mul(t0, a2, P1_)); v(lambda e: e.scalar_tensor_tensor(al1, t0, 2.0, a1, ALU.mult, ALU.add))
            v(lambda e: e.tensor_mul(t0, a3, P2_)); v(lambda e: e.scalar_tensor_tensor(al1, t0, 3.0, al1, ALU.mult, ALU.add))
            v(lambda e: e.tensor_mul(t0, a3, P1_)); v(lambda e: e.scalar_tensor_tensor(al2, t0, 3.0, a2, ALU.mult, ALU.add))
            v(lambda e: e.tensor_mul(t0, ee1, P1_)); v(lambda e: e.tensor_add(zz0, ee0, t0))
            v(lambda e: e.tensor_scalar(um1, P1_, -1.0, None, ALU.add))
            v(lambda e: e.memset(on1, 1.0))
            for (ri, ap) in ((0, al0), (1, al1), (2, al2), (3, a3), (4, um1), (5, on1), (6, zz0), (7, ee1)):
                s.dma("sync", lambda e, ri=ri, ap=ap: e.dma_start(
                    out=rows_dram.ap()[ri, :].rearrange("(p f) -> p f", p=128), in_=ap))
            yield
            for ti in range(CPC // 512):
                yield
                par = ti % 2
                r4 = rr4[:, par * 512:(par + 1) * 512]
                r2u = rr2[:, par * 1024: par * 1024 + 512]
                r2z = rr2[:, par * 1024 + 512: (par + 1) * 1024]
                pw, pu = (pF, pR1)
                csl = slice(ti * 512, (ti + 1) * 512)
                s.dma("sync", lambda e, csl=csl, r4=r4: e.dma_start(out=r4, in_=rows_dram.ap()[0:4, csl]))
                s.dma("sync", lambda e, csl=csl, r2u=r2u: e.dma_start(out=r2u, in_=rows_dram.ap()[4:6, csl]))
                s.dma("sync", lambda e, csl=csl, r2z=r2z: e.dma_start(out=r2z, in_=rows_dram.ap()[6:8, csl]))
                s.add("tensor", lambda e, pw=pw, r4=r4: e.matmul(pw[:, :], b4s[:, :], r4, start=True, stop=True))
                s.add("tensor", lambda e, pu=pu, r2u=r2u: e.matmul(pu[:, :], b2s[:, :], r2u, start=True, stop=True))
                rbuf = C[:, 3072:3584] if par else C[:, 4096:4608]
                r2buf = C[:, 3584:4096] if par else C[:, 4608:5120]
                wbuf = D[:, 11776:12288] if par else D[:, 10752:11264]
                s.add("scalar", lambda e, pu=pu, rbuf=rbuf: e.activation(rbuf, pu[:, :], AF.Relu, bias=0.0, scale=1.0))
                s.add("vector", lambda e, pu=pu, rbuf=rbuf, r2buf=r2buf: e.tensor_mul(r2buf, rbuf, pu[:, :]))
                s.add("tensor", lambda e, pu=pu, r2z=r2z: e.matmul(pu[:, :], b2s[:, :], r2z, start=True, stop=True))
                s.add("vector", lambda e, pu=pu, rbuf=rbuf, r2buf=r2buf: e.tensor_mul(rbuf, r2buf, pu[:, :]))
                s.add("vector", lambda e, pw=pw, rbuf=rbuf, wbuf=wbuf: e.tensor_add(wbuf, pw[:, :], rbuf))
                col0 = ti * 512
                b = col0 // ncol_row
                o = col0 % ncol_row
                s.dma("sync", lambda e, b=b, o=o, wbuf=wbuf: e.dma_start(
                    out=slab_in.ap()[b * 128:(b + 1) * 128, o:o + 512], in_=wbuf))


        def l0_dequant(which, base=None):
            codes, scales = l0c[which], l0s[which]
            base = WOFF if base is None else base
            for dc in range(NDC):
                cb = 0 if dc % 2 == 0 else 4096
                dst = AB[:, base + dc * D_MODEL: base + (dc + 1) * D_MODEL]
                s.dma("gpsimd", lambda e, dc=dc, codes=codes, cb=cb: e.dma_start(
                    out=D[:, cb:cb + 2048], in_=codes[dc * 128:(dc + 1) * 128, :]))
                s.dma("gpsimd", lambda e, dc=dc, scales=scales, cb=cb: e.dma_start(
                    out=D[:, cb + 2048:cb + 4096], in_=scales[dc * 128:(dc + 1) * 128, :]))
                eng_d = "vector" if dc % 2 == 0 else "gpsimd"
                s.add(eng_d, lambda e, dst=dst, cb=cb: e.tensor_mul(dst, D[:, cb:cb + 2048], D[:, cb + 2048:cb + 4096]))

        def l0_project(out_kind, bias_idx):
            if out_kind == "v":
                for tt in range(8):
                    for og in range(4):
                        ps = (pD, pF, pA, pB)[(tt * 4 + og) % 4]
                        for dc in range(NDC):
                            s.add("tensor", lambda e, dc=dc, tt=tt, og=og, ps=ps: e.matmul(
                                ps[:, :], xh_ap(dc)[:, tt * 128:(tt + 1) * 128],
                                AB[:, WOFF + dc * D_MODEL + og * 512: WOFF + dc * D_MODEL + (og + 1) * 512],
                                start=(dc == 0), stop=(dc == NDC - 1)))
                        eb_ = 8192 + (og % 2) * 512
                        s.add("scalar", lambda e, ps=ps, eb_=eb_: e.activation(
                            D[:, eb_:eb_ + 512], ps[:, :], AF.Identity, bias=0.0, scale=1.0))
                        s.dma("sync", lambda e, tt=tt, og=og, eb_=eb_: e.dma_start(
                            out=v_ag_in.ap()[og * 512:(og + 1) * 512, tt * 128:(tt + 1) * 128].rearrange(
                                "(oc t) j -> t oc j", t=128),
                            in_=D[:, eb_:eb_ + 512].rearrange("p (oc j) -> p oc j", j=128)))
                return
            for oc in range(NDC):
                pair = [(pD, pF), (pA, pB)][oc % 2]
                for th in range(2):
                    ps = pair[th]
                    for dc in range(NDC):
                        s.add("tensor", lambda e, dc=dc, oc=oc, th=th, ps=ps: e.matmul(
                            ps[:, :], AB[:, WOFF + dc * D_MODEL + oc * 128: WOFF + dc * D_MODEL + (oc + 1) * 128],
                            xh_ap(dc)[:, th * 512:(th + 1) * 512],
                            start=(dc == 0), stop=(dc == NDC - 1)))
                bcol = biasq[:, bias_idx * NDC + oc: bias_idx * NDC + oc + 1]
                if out_kind == "q":
                    for th in range(2):
                        s.add("scalar", lambda e, oc=oc, bcol=bcol, th=th, ps=pair[th]: e.activation(
                            AB[:, QFOFF + oc * TOK + th * 512: QFOFF + oc * TOK + (th + 1) * 512], ps[:, :],
                            AF.Identity, bias=bcol, scale=1.0))
                else:
                    for th in range(2):
                        s.add("scalar", lambda e, bcol=bcol, th=th, ps=pair[th]: e.activation(
                            D[:, 8192 + th * 512:8704 + th * 512], ps[:, :], AF.Identity, bias=bcol, scale=1.0))
                    if out_kind == "k":
                        s.dma("sync", lambda e, oc=oc: e.dma_start(out=k_ag_in.ap()[oc * 128:(oc + 1) * 128, :], in_=D[:, 8192:9216]))
                    else:
                        for tt in range(8):
                            s.add("tensor", lambda e, tt=tt: e.transpose(
                                pT[:, 0:128], D[:, 8192 + tt * 128: 8320 + tt * 128], idb[:, :]))
                            s.add("vector", lambda e, tt=tt: e.tensor_copy(D[:, 9216 + tt * 128: 9344 + tt * 128], pT[:, 0:128]))
                        for tt in range(8):
                            s.dma("sync", lambda e, oc=oc, tt=tt: e.dma_start(
                                out=v_ag_in.ap()[oc * 128:(oc + 1) * 128, tt * 128:(tt + 1) * 128],
                                in_=D[:, 9216 + tt * 128: 9344 + tt * 128]))

        l0_dequant("q"); l0_project("q", 0)
        l0_dequant("k"); l0_project("k", 1)
        l0_dequant("v"); l0_project("v", 2)
        s.add("gpsimd", lambda e: e.collective_compute(
            "AllGather", ALU.bypass, replica_groups=GROUPS,
            ins=[k_ag_in.ap()], outs=[k_ag_out.ap()]), inc=1)
        s.add("gpsimd", lambda e: e.collective_compute(
            "AllGather", ALU.bypass, replica_groups=GROUPS,
            ins=[v_ag_in.ap()], outs=[v_ag_out.ap()]), inc=1)
        def tern_dequant_gen(nat_src, o_rows, i_cols, dst_tile_fn):
            ng = i_cols // 128
            for ot in range(o_rows // 128):
                yield
                op_ = ot % 2
                cb = 0 if op_ == 0 else 3072
                eb = 32 if op_ == 0 else 96
                s.dma("sync", lambda e, ot=ot, cb=cb: e.dma_start(out=C[:, cb:cb + i_cols], in_=nat_src[ot * 128:(ot + 1) * 128, :]))
                s.add("vector", lambda e, ng=ng, cb=cb, eb=eb: e.tensor_reduce(
                    E[:, eb:eb + ng], C[:, cb:cb + i_cols].rearrange("p (g j) -> p g j", j=128),
                    axis=AX.X, op=ALU.add, apply_absolute_value=True))
                s.add("vector", lambda e, ng=ng, eb=eb: e.tensor_scalar(E[:, eb:eb + ng], E[:, eb:eb + ng], 1.0 / 128.0, TERN_EPS, ALU.mult, ALU.max))
                s.add("vector", lambda e, ng=ng, eb=eb: e.tensor_scalar(E[:, eb + 32:eb + 32 + ng], E[:, eb:eb + ng], 0.5, None, ALU.mult))
                for g in range(ng):
                    gp = g % 2
                    eng_t = "vector" if gp == 0 else "gpsimd"
                    qb = 2048 + gp * 128 if op_ == 0 else 5120 + gp * 128
                    db_ = 11264 + (2 * op_ + gp) * 128
                    w_t = C[:, cb + g * 128: cb + (g + 1) * 128]
                    s.add(eng_t, lambda e, w_t=w_t, g=g, qb=qb, eb=eb: e.tensor_single_scalar(
                        C[:, qb:qb + 128], w_t, E[:, eb + 32 + g:eb + 33 + g], ALU.is_gt))
                    nb = 2432 + gp * 128 if op_ == 0 else 5504 + gp * 128
                    s.add(eng_t, lambda e, w_t=w_t, g=g, nb=nb, eb=eb: e.tensor_scalar(
                        C[:, nb:nb + 128], w_t, -1.0, E[:, eb + 32 + g:eb + 33 + g], ALU.mult, ALU.is_gt))
                    s.add(eng_t, lambda e, qb=qb, nb=nb: e.tensor_sub(C[:, qb:qb + 128], C[:, qb:qb + 128], C[:, nb:nb + 128]))
                    s.add(eng_t, lambda e, g=g, qb=qb, db_=db_, eb=eb: e.tensor_scalar_mul(D[:, db_:db_ + 128], C[:, qb:qb + 128], E[:, eb + g:eb + 1 + g]))
                    s.add("tensor", lambda e, db_=db_, gp=gp: e.transpose(pT[:, gp * 128:(gp + 1) * 128], D[:, db_:db_ + 128], idb[:, :]))
                    s.add("vector", lambda e, g=g, ot=ot, gp=gp: e.tensor_copy(dst_tile_fn(g, ot), pT[:, gp * 128:(gp + 1) * 128]))

        def _fill_gens():
            for pi in range(4):
                nat = (l1q_nat, l1k_nat, l1v_nat, l1o_nat)[pi]
                if pi < 3:
                    yield from tern_dequant_gen(nat, 512, D_MODEL,
                                                lambda ic, ot: D[:, ic * 512 + ot * 128: ic * 512 + (ot + 1) * 128])
                else:
                    yield from tern_dequant_gen(nat, D_MODEL, 512,
                                                lambda ic, ot: D[:, ic * 2048 + ot * 128: ic * 2048 + (ot + 1) * 128])
                s.dma("sync", lambda e, pi=pi: e.dma_start(out=l1w_sc.ap()[pi], in_=D[:, 0:8192]))
            for (wk, si, nr) in (("g", g_sl_in, D_MODEL), ("u", u_sl_in, D_MODEL), ("d", d_sl_in, INTER)):
                yield from recon_gen(wk, si, nr)
            for (si, so) in ((g_sl_in, g_sl), (u_sl_in, u_sl), (d_sl_in, d_sl)):
                s.add("gpsimd", lambda e, si=si, so=so: e.collective_compute(
                    "AllGather", ALU.bypass, replica_groups=[[0, 1, 2, 3, 4, 5, 6, 7]],
                    ins=[si.ap()], outs=[so.ap()]), inc=1)
            yield

        _gen = _fill_gens()
        _pre = []
        for e_ in range(NEXP):
            for (dst, srcw) in ((w1B, w1R), (w3B, w3R), (w2B, w2R)):
                _pre.append((dst.ap()[e_], srcw[e_]))
        _pre.append((sw1B.ap(), sw1R))
        _pre.append((sw3B.ap(), sw3R))
        _pre.append((sw2B.ap(), sw2R))
        _state = {"gen_done": False, "pre_i": 0, "pre_on": False}

        def pump(k=1):
            for _ in range(k):
                if not _state["gen_done"]:
                    try:
                        next(_gen)
                    except StopIteration:
                        _state["gen_done"] = True
                if _state["pre_on"] and _state["pre_i"] < len(_pre):
                    dst, srcw = _pre[_state["pre_i"]]
                    _state["pre_i"] += 1
                    s.dma("gpsimd", lambda e, dst=dst, srcw=srcw: e.dma_start(
                        out=dst.rearrange("a p f -> (a p) f"), in_=srcw.rearrange("a p f -> (a p) f")))

        def pump_drain():
            _state["pre_on"] = True
            while not _state["gen_done"] or _state["pre_i"] < len(_pre):
                pump(1)
        _state["pre_on"] = True

        if debug_tap == "qkv0":
            dump_dbg_bf(lambda oc: AB[:, QFOFF + oc * TOK: QFOFF + (oc + 1) * TOK])
            return nc

        ATOFF = 0
        KVBUF = 16384
        for h in range(NH):
            kb = KVBUF + (h % 2) * 8192
            for r in range(4):
                s.dma("sync", lambda e, h=h, r=r, kb=kb: e.dma_start(
                    out=AB[:, kb + r * TOK: kb + (r + 1) * TOK],
                    in_=k_ag_out.ap()[r * D_MODEL + h * 128: r * D_MODEL + (h + 1) * 128, :]))
                s.dma("gpsimd", lambda e, h=h, r=r, kb=kb: e.dma_start(
                    out=AB[:, kb + 4096 + r * TOK: kb + 4096 + (r + 1) * TOK],
                    in_=v_ag_out.ap()[r * D_MODEL + h * 128: r * D_MODEL + (h + 1) * 128, :]))
            for qc in range(2):
                qsl = slice(qc * 512, (qc + 1) * 512)
                pvac = pC if (h * 2 + qc) % 2 == 0 else pD
                lrow = pR2[0:1, :] if (h * 2 + qc) % 2 == 0 else pR2[32:33, :]
                def emit_pv(kt, pvac=pvac, lrow=lrow, kb=kb):
                    pr = (kt // 2) % 2
                    pts = D[:, 8192 + pr * 1024 + (kt % 2) * 512: 8192 + pr * 1024 + (kt % 2 + 1) * 512]
                    s.add("tensor", lambda e, kt=kt, pts=pts, kb=kb: e.matmul(
                        pvac[:, :], AB[:, kb + 4096 + kt * 128: kb + 4096 + (kt + 1) * 128], pts,
                        start=(kt == 0), stop=(kt == 31)))
                    s.add("tensor", lambda e, kt=kt, pts=pts: e.matmul(
                        lrow, onb[:, :], pts, start=(kt == 0), stop=(kt == 31)))
                for kt in range(32):
                    ps = pAB[:, (kt % 2) * 512:(kt % 2 + 1) * 512]
                    s.add("tensor", lambda e, h=h, kt=kt, qsl=qsl, ps=ps, kb=kb: e.matmul(
                        ps, AB[:, kb + kt * 128: kb + (kt + 1) * 128],
                        AB[:, QFOFF + h * TOK: QFOFF + (h + 1) * TOK][:, qsl], start=True, stop=True))
                    if kt % 2 == 1:
                        pr = (kt // 2) % 2
                        ptsp = D[:, 8192 + pr * 1024: 8192 + (pr + 1) * 1024]
                        s.add("scalar", lambda e, ptsp=ptsp: e.activation(
                            ptsp, pAB[:, 0:1024], AF.Exp, bias=shc[:, 0:1], scale=C_ATT))
                        if kt > 1:
                            emit_pv(kt - 3)
                            emit_pv(kt - 2)
                emit_pv(30)
                emit_pv(31)
                tslot = E[0:1, 0:512] if (h * 2 + qc) % 2 == 0 else E[0:1, 512:1024]
                bslot = slice(5120, 5632) if (h * 2 + qc) % 2 == 0 else slice(5632, 6144)
                palt = pD if (h * 2 + qc) % 2 == 0 else pC
                s.add("vector", lambda e, lrow=lrow, tslot=tslot: e.reciprocal(tslot, lrow))
                s.add("tensor", lambda e, tslot=tslot, palt=palt: e.matmul(palt[:, :], orf[:, :], tslot, start=True, stop=True))
                s.add("scalar", lambda e, bslot=bslot, palt=palt: e.activation(C[:, bslot], palt[:, :], AF.Identity, bias=0.0, scale=1.0))
                s.add("vector", lambda e, h=h, qsl=qsl, pvac=pvac, bslot=bslot: e.tensor_mul(
                    AB[:, ATOFF + h * TOK: ATOFF + (h + 1) * TOK][:, qsl], pvac[:, :], C[:, bslot]))
                s.add("vector", lambda e, h=h, qsl=qsl: e.tensor_scalar_add(
                    AB[:, ATOFF + h * TOK: ATOFF + (h + 1) * TOK][:, qsl],
                    AB[:, ATOFF + h * TOK: ATOFF + (h + 1) * TOK][:, qsl],
                    biasq[:, 2 * NDC + h: 2 * NDC + h + 1]))
                pump(5)

        if debug_tap == "attn0":
            dump_dbg_bf(lambda oc: AB[:, ATOFF + oc * TOK: ATOFF + (oc + 1) * TOK])
            return nc

        WO2 = 16384
        l0_dequant("o", base=WO2)
        for oc in range(NDC):
            pump(3)
            pair = [(pD, pF), (pA, pB)][oc % 2]
            for th in range(2):
                ps = pair[th]
                for dc in range(NDC):
                    s.add("tensor", lambda e, dc=dc, oc=oc, th=th, ps=ps: e.matmul(
                        ps[:, :], AB[:, WO2 + dc * D_MODEL + oc * 128: WO2 + dc * D_MODEL + (oc + 1) * 128],
                        AB[:, ATOFF + dc * TOK: ATOFF + (dc + 1) * TOK][:, th * 512:(th + 1) * 512],
                        start=(dc == 0), stop=(dc == NDC - 1)))
            bcol = biasq[:, 3 * NDC + oc: 3 * NDC + oc + 1]
            for th in range(2):
                s.add("scalar", lambda e, bcol=bcol, th=th, ps=pair[th]: e.activation(
                    C[:, 3072 + th * 512:3584 + th * 512], ps[:, :], AF.Identity, bias=bcol, scale=1.0))
            s.add("gpsimd", lambda e: e.tensor_mul(C[:, 4096:5120], C[:, 3072:4096], C[:, 3072:4096]))
            stats_mm(C[:, 3072:4096], C[:, 4096:5120], oc == 0, oc == NDC - 1)
            s.dma("gpsimd", lambda e, oc=oc: e.dma_start(out=sub_dram.ap()[oc, :, :], in_=C[:, 3072:4096]))
        ln_rows()
        ln_residual(sub_dram)
        pump_drain()
        if debug_tap == "x1":
            s.dma("sync", lambda e: e.dma_start(out=dbg[:, :, :], in_=x_dram.ap()[:, :, :]))
            s.add("gpsimd", lambda e: e.engine_nop())
            s.emit(nc, st)
            return nc

        ln_stats_dram_f32(x_dram)
        ln_apply(xh_ap, x_dram)
        for th in range(2):
            for dc in range(NDC):
                s.add("tensor", lambda e, dc=dc, th=th: e.matmul(
                    pR2[0:8, :], gtb[:, dc * NEXP:(dc + 1) * NEXP],
                    xh_ap(dc)[:, th * 512:(th + 1) * 512], start=(dc == 0), stop=(dc == NDC - 1)))
            s.add("scalar", lambda e, th=th: e.activation(C[0:8, 4096 + th * 512:4608 + th * 512], pR2[0:8, :], AF.Sigmoid, bias=0.0, scale=1.0))
        s.add("vector", lambda e: e.tensor_scalar(C[0:8, 5120:6144], C[0:8, 4096:5120], ebc[:, 0:1], None, ALU.add))
        for tt in range(8):
            s.add("tensor", lambda e, tt=tt: e.transpose(pR1[:, 0:8], C[0:8, 5120 + tt * 128:5248 + tt * 128], idf[0:8, 0:8]))
            s.add("vector", lambda e: e.tensor_copy(C[:, 2048:2056], pR1[:, 0:8]))
            s.add("tensor", lambda e, tt=tt: e.transpose(pR1[:, 8:16], C[0:8, 4096 + tt * 128:4224 + tt * 128], idf[0:8, 0:8]))
            s.add("vector", lambda e: e.tensor_copy(C[:, 2056:2064], pR1[:, 8:16]))
            s.add("vector", lambda e: e.reduce_max(C[:, 2064:2065], C[:, 2048:2056], axis=AX.X))
            s.add("vector", lambda e: e.tensor_single_scalar(C[:, 2072:2080], C[:, 2048:2056], C[:, 2064:2065], ALU.is_equal))
            s.add("vector", lambda e: e.scalar_tensor_tensor(C[:, 2080:2088], C[:, 2072:2080], -1e30, C[:, 2048:2056], ALU.mult, ALU.add))
            s.add("vector", lambda e: e.reduce_max(C[:, 2065:2066], C[:, 2080:2088], axis=AX.X))
            s.add("vector", lambda e: e.tensor_single_scalar(C[:, 2072:2080], C[:, 2048:2056], C[:, 2065:2066], ALU.is_ge))
            s.add("vector", lambda e: e.tensor_mul(C[:, 2072:2080], C[:, 2072:2080], C[:, 2056:2064]))
            s.add("tensor", lambda e: e.transpose(pR1[0:8, 0:128], C[:, 2072:2080], idf[:, :]))
            s.add("vector", lambda e, tt=tt: e.tensor_copy(C[0:8, tt * 128:(tt + 1) * 128], pR1[0:8, 0:128]))
        if debug_tap == "ms":
            s.add("vector", lambda e: e.tensor_copy(C[0:8, 2048:3072], C[0:8, 0:1024]))
            s.dma("sync", lambda e: e.dma_start(out=dbg[0, 0:8, :], in_=C[0:8, 2048:3072]))
            s.dma("sync", lambda e: e.dma_start(out=dbg[1, 0:8, :], in_=C[0:8, 4096:5120]))
            s.dma("sync", lambda e: e.dma_start(out=dbg[2, 0:8, :], in_=C[0:8, 5120:6144]))
            s.add("gpsimd", lambda e: e.engine_nop())
            s.emit(nc, st)
            return nc
        XEOFF = 49152
        for th in range(2):
            tsl = slice(th * 512, (th + 1) * 512)
            MSB = 61440
            for ei in range(8):
                s.add("tensor", lambda e, ei=ei, tsl=tsl: e.matmul(
                    pR1[:, :], slb[:, ei * 128:(ei + 1) * 128], C[0:8, 0:1024][:, tsl], start=True, stop=True))
                s.add("scalar", lambda e, ei=ei: e.activation(AB[:, MSB + ei * 512: MSB + (ei + 1) * 512],
                                                              pR1[:, :], AF.Identity, bias=0.0, scale=1.0))
            for ei in range(9):
                he_off = ei * 8 * 512 if ei < 8 else 57344
                if ei < 8:
                    for dc in range(NDC):
                        eng_x = "vector" if dc % 2 == 0 else "gpsimd"
                        s.add(eng_x, lambda e, dc=dc, tsl=tsl, ei=ei: e.tensor_mul(
                            AB[:, XEOFF + dc * 512: XEOFF + (dc + 1) * 512], xh_ap(dc)[:, tsl],
                            AB[:, MSB + ei * 512: MSB + (ei + 1) * 512]))
                    xe_list = [AB[:, XEOFF + dc * 512: XEOFF + (dc + 1) * 512] for dc in range(NDC)]
                else:
                    xe_list = [xh_ap(dc)[:, tsl] for dc in range(NDC)]
                w1_src = w1B.ap()[ei] if ei < 8 else sw1B.ap()
                w3_src = w3B.ap()[ei] if ei < 8 else sw3B.ap()
                for hidc in range(8):
                    w1b = 512 if hidc % 2 == 0 else 5120
                    w3b = 2560 if hidc % 2 == 0 else 7168
                    s.dma("sync", lambda e, w=w1_src, hidc=hidc, w1b=w1b: e.dma_start(
                        out=D[:, w1b:w1b + 2048], in_=w[hidc]))
                    s.dma("gpsimd", lambda e, w=w3_src, hidc=hidc, w3b=w3b: e.dma_start(
                        out=D[:, w3b:w3b + 2048], in_=w[hidc]))
                    u1p = pA if hidc % 2 == 0 else pC
                    u3p = pB if hidc % 2 == 0 else pF
                    for dc in range(NDC):
                        s.add("tensor", lambda e, dc=dc, xe=xe_list[dc], w1b=w1b, u1p=u1p: e.matmul(
                            u1p[:, :], D[:, w1b + dc * 128: w1b + (dc + 1) * 128], xe,
                            start=(dc == 0), stop=(dc == NDC - 1)))
                        s.add("tensor", lambda e, dc=dc, xe=xe_list[dc], w3b=w3b, u3p=u3p: e.matmul(
                            u3p[:, :], D[:, w3b + dc * 128: w3b + (dc + 1) * 128], xe,
                            start=(dc == 0), stop=(dc == NDC - 1)))
                    s.add("scalar", lambda e, u1p=u1p: e.activation(D[:, 4608:5120], u1p[:, :], AF.Silu, bias=0.0, scale=1.0))
                    s.add("vector", lambda e, he_off=he_off, hidc=hidc, u3p=u3p: e.tensor_mul(
                        AB[:, he_off + hidc * 512: he_off + (hidc + 1) * 512], u3p[:, :], D[:, 4608:5120]))
            for dc2 in range(NDC):
                dacc = pC if dc2 % 2 == 0 else pD
                cnt = 0
                for ei in range(9):
                    w2_src = w2B.ap()[ei] if ei < 8 else sw2B.ap()
                    he_off = ei * 8 * 512 if ei < 8 else 57344
                    wb = 512 + ((dc2 * 9 + ei) % 2) * 1024
                    s.dma("sync" if ei % 2 == 0 else "gpsimd", lambda e, w=w2_src, dc2=dc2, wb=wb: e.dma_start(
                        out=D[:, wb:wb + 1024], in_=w[dc2]))
                    for hc in range(8):
                        s.add("tensor", lambda e, hc=hc, he_off=he_off, cnt=cnt, wb=wb, dacc=dacc: e.matmul(
                            dacc[:, :], D[:, wb + hc * 128: wb + (hc + 1) * 128],
                            AB[:, he_off + hc * 512: he_off + (hc + 1) * 512],
                            start=(cnt == 0), stop=(cnt == 71)))
                        cnt += 1
                ch = C[:, 3072:4096][:, tsl]
                s.add("scalar", lambda e, ch=ch, dacc=dacc: e.activation(ch, dacc[:, :], AF.Identity, bias=0.0, scale=1.0))
                s.add("gpsimd", lambda e, ch=ch: e.tensor_mul(C[:, 4096:4608], ch, ch))
                rS = pR1[0:1, :] if th == 0 else pR1[32:33, :]
                rQ = pR2[0:1, :] if th == 0 else pR2[32:33, :]
                s.add("tensor", lambda e, ch=ch, dc2=dc2, rS=rS: e.matmul(
                    rS, onf[:, :], ch, start=(dc2 == 0), stop=(dc2 == NDC - 1)))
                s.add("tensor", lambda e, dc2=dc2, rQ=rQ: e.matmul(
                    rQ, onf[:, :], C[:, 4096:4608], start=(dc2 == 0), stop=(dc2 == NDC - 1)))
                s.dma("gpsimd", lambda e, dc2=dc2, th=th, ch=ch: e.dma_start(
                    out=sub_dram.ap()[dc2, :, th * 512:(th + 1) * 512], in_=ch))
            tslr = slice(th * 512, (th + 1) * 512)
            rS = pR1[0:1, :] if th == 0 else pR1[32:33, :]
            rQ = pR2[0:1, :] if th == 0 else pR2[32:33, :]
            s.add("vector", lambda e, tslr=tslr, rS=rS: e.tensor_scalar(MEAN[:, tslr], rS, 1.0 / 2048.0, None, ALU.mult))
            s.add("vector", lambda e, tslr=tslr, rQ=rQ: e.tensor_scalar(T1[:, tslr], rQ, 1.0 / 2048.0, None, ALU.mult))
        if debug_tap == "moe":
            for dc in range(NDC):
                s.dma("sync", lambda e, dc=dc: e.dma_start(out=D[:, 10240:11264], in_=sub_dram.ap()[dc, :, :]))
                s.add("vector", lambda e: e.tensor_copy(C[:, 3072:4096], D[:, 10240:11264]))
                s.dma("sync", lambda e, dc=dc: e.dma_start(out=dbg[dc, :, :], in_=C[:, 3072:4096]))
            s.add("gpsimd", lambda e: e.engine_nop())
            s.emit(nc, st)
            return nc
        s.add("vector", lambda e: e.tensor_mul(T2[:, :], MEAN[:, :], MEAN[:, :]))
        s.add("vector", lambda e: e.tensor_sub(T1[:, :], T1[:, :], T2[:, :]))
        s.add("vector", lambda e: e.tensor_scalar(T1[:, :], T1[:, :], LN_EPS, None, ALU.add))
        s.add("scalar", lambda e: e.activation(T2[:, :], T1[:, :], AF.Ln, bias=0.0, scale=1.0))
        s.add("scalar", lambda e: e.activation(RSTD[:, :], T2[:, :], AF.Exp, bias=0.0, scale=-0.5))
        ln_residual(sub_dram)
        if debug_tap == "x2":
            s.dma("sync", lambda e: e.dma_start(out=dbg[:, :, :], in_=x_dram.ap()[:, :, :]))
            s.add("gpsimd", lambda e: e.engine_nop())
            s.emit(nc, st)
            return nc

        ln_stats_dram_f32(x_dram)
        ln_apply(xh_ap, x_dram)
        for dc in range(NDC):
            s.dma("sync", lambda e, dc=dc: e.dma_start(out=xh_ag_in.ap()[dc * 128:(dc + 1) * 128, :], in_=xh_ap(dc)))
        s.add("gpsimd", lambda e: e.collective_compute(
            "AllGather", ALU.bypass, replica_groups=GROUPS,
            ins=[xh_ag_in.ap()], outs=[xh_ag_out.ap()]), inc=1)

        QO, KO, VO, XBO = 0, 16384, 32768, 49152
        for (pi, dsto) in ((0, QO), (1, KO), (2, VO)):
            s.dma("sync", lambda e, pi=pi: e.dma_start(out=D[:, 0:8192], in_=l1w_sc.ap()[pi]))
            for rb in range(4):
                for dc in range(NDC):
                    s.dma("sync", lambda e, rb=rb, dc=dc: e.dma_start(
                        out=AB[:, XBO + dc * TOK: XBO + (dc + 1) * TOK],
                        in_=xh_ag_out.ap()[rb * D_MODEL + dc * 128: rb * D_MODEL + (dc + 1) * 128, :]))
                for fc in range(4):
                    pair = [(pD, pF), (pA, pB)][fc % 2]
                    for th in range(2):
                        ps = pair[th]
                        for dc in range(NDC):
                            s.add("tensor", lambda e, dc=dc, fc=fc, ps=ps, th=th: e.matmul(
                                ps[:, :], D[:, dc * 512 + fc * 128: dc * 512 + (fc + 1) * 128],
                                AB[:, XBO + dc * TOK: XBO + (dc + 1) * TOK][:, th * 512:(th + 1) * 512],
                                start=(dc == 0), stop=(dc == NDC - 1)))
                    for th in range(2):
                        s.add("scalar", lambda e, dsto=dsto, fc=fc, rb=rb, th=th, ps=pair[th]: e.activation(
                            AB[:, dsto + fc * SEQ + rb * TOK + th * 512: dsto + fc * SEQ + rb * TOK + (th + 1) * 512],
                            ps[:, :], AF.Identity, bias=0.0, scale=1.0))

        s.dma("gpsimd", lambda e: e.dma_start(out=D[0:64, 0:4096], in_=cos_in[:, :]))
        s.dma("gpsimd", lambda e: e.dma_start(out=D[0:64, 4096:8192], in_=sin_in[:, :]))
        cosL = D[0:64, 0:4096]
        sinL = D[0:64, 4096:8192]
        Tlo = AB[0:64, XBO: XBO + SEQ]
        Aq = AB[0:64, XBO + SEQ: XBO + 2 * SEQ]
        Bq = AB[0:64, XBO + 2 * SEQ: XBO + 3 * SEQ]
        C2q = AB[0:64, XBO + 3 * SEQ: XBO + 4 * SEQ]
        for base in (QO, KO):
            for fc in range(4):
                x1 = AB[0:64, base + fc * SEQ: base + (fc + 1) * SEQ]
                x2hi = AB[64:128, base + fc * SEQ: base + (fc + 1) * SEQ]
                s.dma("sync", lambda e, x2hi=x2hi: e.dma_start(out=Tlo, in_=x2hi))
                s.add("vector", lambda e, x1=x1: e.tensor_mul(Aq, x1, cosL))
                s.add("vector", lambda e: e.tensor_mul(C2q, Tlo, sinL))
                s.add("vector", lambda e: e.tensor_sub(Aq, Aq, C2q))
                s.add("vector", lambda e, x1=x1: e.tensor_mul(Bq, x1, sinL))
                s.add("vector", lambda e: e.tensor_mul(C2q, Tlo, cosL))
                s.add("vector", lambda e: e.tensor_add(Bq, Bq, C2q))
                s.add("vector", lambda e, x1=x1: e.tensor_copy(x1, Aq))
                s.dma("sync", lambda e, x2hi=x2hi: e.dma_start(out=x2hi, in_=Bq))
        for fc in range(4):
            for kt in range(32):
                s.add("tensor", lambda e, fc=fc, kt=kt: e.transpose(
                    pT[:, 0:128], AB[:, VO + fc * SEQ + kt * 128: VO + fc * SEQ + (kt + 1) * 128], idb[:, :]))
                s.add("vector", lambda e, fc=fc, kt=kt: e.tensor_copy(
                    AB[:, XBO + kt * 512 + fc * 128: XBO + kt * 512 + (fc + 1) * 128], pT[:, 0:128]))
        VTO = XBO

        for fc in range(4):
            for qc in range(8):
                qsl = slice(qc * 512, (qc + 1) * 512)
                nkt = 4 * qc + 4
                pvac = pC if (fc * 8 + qc) % 2 == 0 else pD
                lrow = pR2[0:1, :] if (fc * 8 + qc) % 2 == 0 else pR2[32:33, :]
                def emit_pv1(kt, fc=fc, nkt=nkt, pvac=pvac, lrow=lrow):
                    pr = (kt // 2) % 2
                    pts = D[:, 8192 + pr * 1024 + (kt % 2) * 512: 8192 + pr * 1024 + (kt % 2 + 1) * 512]
                    s.add("tensor", lambda e, kt=kt, fc=fc, pts=pts, nkt=nkt: e.matmul(
                        pvac[:, :], AB[:, VTO + kt * 512 + fc * 128: VTO + kt * 512 + (fc + 1) * 128], pts,
                        start=(kt == 0), stop=(kt == nkt - 1)))
                    s.add("tensor", lambda e, kt=kt, pts=pts, nkt=nkt: e.matmul(
                        lrow, onb[:, :], pts, start=(kt == 0), stop=(kt == nkt - 1)))
                for kt in range(nkt):
                    ps = pAB[:, (kt % 2) * 512:(kt % 2 + 1) * 512]
                    s.add("tensor", lambda e, fc=fc, kt=kt, qsl=qsl, ps=ps: e.matmul(
                        ps, AB[:, KO + fc * SEQ + kt * 128: KO + fc * SEQ + (kt + 1) * 128],
                        AB[:, QO + fc * SEQ: QO + (fc + 1) * SEQ][:, qsl], start=True, stop=True))
                    j = kt - 4 * qc
                    if j >= 0:
                        s.add("vector", lambda e, ps=ps, j=j: e.tensor_add(ps, ps, msk[:, j * 512:(j + 1) * 512]))
                    if kt % 2 == 1:
                        pr = (kt // 2) % 2
                        ptsp = D[:, 8192 + pr * 1024: 8192 + (pr + 1) * 1024]
                        s.add("scalar", lambda e, ptsp=ptsp: e.activation(ptsp, pAB[:, 0:1024], AF.Exp, bias=shc[:, 0:1], scale=C_ATT))
                        if kt > 1:
                            emit_pv1(kt - 3)
                            emit_pv1(kt - 2)
                emit_pv1(nkt - 2)
                emit_pv1(nkt - 1)
                tslot = E[0:1, 0:512] if (fc * 8 + qc) % 2 == 0 else E[0:1, 512:1024]
                bslot = slice(5120, 5632) if (fc * 8 + qc) % 2 == 0 else slice(5632, 6144)
                s.add("vector", lambda e, lrow=lrow, tslot=tslot: e.reciprocal(tslot, lrow))
                s.add("tensor", lambda e, tslot=tslot: e.matmul(pR1[:, :], orf[:, :], tslot, start=True, stop=True))
                s.add("scalar", lambda e, bslot=bslot: e.activation(C[:, bslot], pR1[:, :], AF.Identity, bias=0.0, scale=1.0))
                s.add("vector", lambda e, fc=fc, qsl=qsl, pvac=pvac, bslot=bslot: e.tensor_mul(
                    AB[:, VO + fc * SEQ: VO + (fc + 1) * SEQ][:, qsl], pvac[:, :], C[:, bslot]))

        s.dma("sync", lambda e: e.dma_start(out=D[:, 0:8192], in_=l1w_sc.ap()[3]))
        for tb in range(4):
            for oc in range(NDC):
                pair = [(pD, pF), (pA, pB)][oc % 2]
                for th in range(2):
                    ps = pair[th]
                    for ic in range(4):
                        s.add("tensor", lambda e, ic=ic, oc=oc, tb=tb, th=th, ps=ps: e.matmul(
                            ps[:, :], D[:, ic * 2048 + oc * 128: ic * 2048 + (oc + 1) * 128],
                            AB[:, VO + ic * SEQ + tb * TOK + th * 512: VO + ic * SEQ + tb * TOK + (th + 1) * 512],
                            start=(ic == 0), stop=(ic == 3)))
                db8 = 8192 if oc % 2 == 0 else 9216
                for th in range(2):
                    s.add("scalar", lambda e, th=th, ps=pair[th], db8=db8: e.activation(
                        D[:, db8 + th * 512:db8 + (th + 1) * 512], ps[:, :], AF.Identity, bias=0.0, scale=1.0))
                s.dma("sync", lambda e, tb=tb, oc=oc, db8=db8: e.dma_start(
                    out=rs_in.ap()[tb * D_MODEL + oc * 128: tb * D_MODEL + (oc + 1) * 128, :], in_=D[:, db8:db8 + 1024]))
        s.add("gpsimd", lambda e: e.collective_compute(
            "ReduceScatter", ALU.add, replica_groups=GROUPS,
            ins=[rs_in.ap()], outs=[rs_out.ap()]), inc=1)
        ln_stats_dram_bf(rs_out)
        ln_residual_bf2d(rs_out)
        if debug_tap == "x3":
            s.dma("sync", lambda e: e.dma_start(out=dbg[:, :, :], in_=x_dram.ap()[:, :, :]))
            s.add("gpsimd", lambda e: e.engine_nop())
            s.emit(nc, st)
            return nc

        XH2 = 32768

        def xh2_ap(dc):
            return AB[:, XH2 + dc * TOK: XH2 + (dc + 1) * TOK]

        ln_stats_dram_f32(x_dram)
        ln_apply(xh2_ap, x_dram)


        for oh in range(2):
            for ocg in range(32):
                og = oh * 32 + ocg
                gb = 0 if ocg % 2 == 0 else 5120
                ub = 2048 if ocg % 2 == 0 else 7168
                s.dma("sync", lambda e, og=og, gb=gb: e.dma_start(
                    out=D[:, gb:gb + 2048], in_=g_sl.ap()[og * 128:(og + 1) * 128, :]))
                s.dma("gpsimd", lambda e, og=og, ub=ub: e.dma_start(
                    out=D[:, ub:ub + 2048], in_=u_sl.ap()[og * 128:(og + 1) * 128, :]))
                gpair = [(pD, pF), (pC, pR1)][ocg % 2]
                for th in range(2):
                    psg = gpair[th]
                    psu = pA if th == 0 else pB
                    for dc in range(NDC):
                        s.add("tensor", lambda e, dc=dc, th=th, psg=psg, gb=gb: e.matmul(
                            psg[:, :], D[:, gb + dc * 128: gb + (dc + 1) * 128], xh2_ap(dc)[:, th * 512:(th + 1) * 512],
                            start=(dc == 0), stop=(dc == NDC - 1)))
                        s.add("tensor", lambda e, dc=dc, th=th, psu=psu, ub=ub: e.matmul(
                            psu[:, :], D[:, ub + dc * 128: ub + (dc + 1) * 128], xh2_ap(dc)[:, th * 512:(th + 1) * 512],
                            start=(dc == 0), stop=(dc == NDC - 1)))
                for th in range(2):
                    s.add("scalar", lambda e, th=th, psg=gpair[th]: e.activation(
                        D[:, 4096 + th * 512:4608 + th * 512], psg[:, :], AF.Silu, bias=0.0, scale=1.0))
                s.add("vector", lambda e, ocg=ocg: e.tensor_mul(AB[:, ocg * TOK: ocg * TOK + 512], pA[:, :], D[:, 4096:4608]))
                s.add("vector", lambda e, ocg=ocg: e.tensor_mul(AB[:, ocg * TOK + 512: (ocg + 1) * TOK], pB[:, :], D[:, 4608:5120]))
            for dc2 in range(NDC):
                db = 0 if dc2 % 2 == 0 else 4096
                s.dma("sync" if dc2 % 2 == 0 else "gpsimd", lambda e, dc2=dc2, oh=oh, db=db: e.dma_start(
                    out=D[:, db:db + 4096],
                    in_=d_sl.ap()[dc2 * 128:(dc2 + 1) * 128, oh * 4096:(oh + 1) * 4096]))
                dpair = [(pD, pF), (pC, pR1)][dc2 % 2]
                for ocg in range(32):
                    for th in range(2):
                        s.add("tensor", lambda e, ocg=ocg, th=th, db=db, ps=dpair[th]: e.matmul(
                            ps[:, :], D[:, db + ocg * 128: db + (ocg + 1) * 128],
                            AB[:, ocg * TOK + th * 512: ocg * TOK + (th + 1) * 512],
                            start=(ocg == 0), stop=(ocg == 31)))
                for th in range(2):
                    s.add("scalar", lambda e, th=th, ps=dpair[th]: e.activation(
                        C[:, 3072 + th * 512:3584 + th * 512], ps[:, :], AF.Identity, bias=0.0, scale=1.0))
                if oh == 0:
                    s.dma("gpsimd", lambda e, dc2=dc2: e.dma_start(out=ffn_acc.ap()[dc2, :, :], in_=C[:, 3072:4096]))
                else:
                    s.dma("gpsimd", lambda e, dc2=dc2: e.dma_start(out=ffn_acc.ap()[dc2, :, :], in_=C[:, 3072:4096], accum_op=ALU.add))
        ln_stats_dram_f32(ffn_acc)
        ln_residual(ffn_acc, sub_is_f32=True, final=True)

        s.add("gpsimd", lambda e: e.engine_nop())
        s.emit(nc, st)
    return nc


def build(debug_tap=None):
    cap = Sched(capture=True)
    build_program(debug_tap, _sched=cap)
    dg = compute_dep_graph(cap.ops, cap.footprints)
    s2 = Sched(dep_graph=dg)
    return build_program(debug_tap, _sched=s2)


_cache = {}


def prep_inputs(inputs):
    consts = host_constants()
    f32 = np.float32
    asf = lambda a: np.ascontiguousarray(np.asarray(a, dtype=f32))
    x = np.asarray(inputs["L0_attn_x"], f32)
    shared = {}
    for w in "qkvo":
        shared[f"w{w}_codes"] = asf(inputs[f"L0_attn_w{w}_codes"])
        shared[f"w{w}_scales"] = asf(np.repeat(np.asarray(inputs[f"L0_attn_w{w}_scales"]), 32, axis=0))
    bias = np.stack([inputs["L0_attn_bq"], inputs["L0_attn_bk"], inputs["L0_attn_bv"],
                     inputs["L0_attn_bo"]], axis=0).astype(f32)
    shared["bias_qkvo"] = asf(bias.reshape(4, NDC, 128).transpose(2, 0, 1).reshape(128, 4 * NDC))
    shared["gateT"] = asf(np.asarray(inputs["L0_ffn_gate"]).T.reshape(NDC, 128, NEXP).transpose(1, 0, 2).reshape(128, NDC * NEXP))
    shared["ebias"] = asf(inputs["L0_ffn_expert_bias"])
    def blk_up(wT):
        return np.ascontiguousarray(wT.reshape(16, 128, 8, 128).transpose(2, 1, 0, 3).reshape(8, 128, 2048))

    def blk_dn(wT):
        return np.ascontiguousarray(wT.reshape(8, 128, 16, 128).transpose(2, 1, 0, 3).reshape(16, 128, 1024))
    w1T_ = np.asarray(inputs["L0_ffn_w1"]).transpose(0, 2, 1).astype(f32)
    w3T_ = np.asarray(inputs["L0_ffn_w3"]).transpose(0, 2, 1).astype(f32)
    w2T_ = np.asarray(inputs["L0_ffn_w2"]).transpose(0, 2, 1).astype(f32)
    shared["w1R"] = np.stack([blk_up(w1T_[e]) for e in range(8)])
    shared["w3R"] = np.stack([blk_up(w3T_[e]) for e in range(8)])
    shared["w2R"] = np.stack([blk_dn(w2T_[e]) for e in range(8)])
    shared["sw1R"] = blk_up(np.asarray(inputs["L0_ffn_shared_w1"]).T.astype(f32))
    shared["sw3R"] = blk_up(np.asarray(inputs["L0_ffn_shared_w3"]).T.astype(f32))
    shared["sw2R"] = blk_dn(np.asarray(inputs["L0_ffn_shared_w2"]).T.astype(f32))
    shared.update(consts)
    s0_gu = spline_cols("gu")
    s0_d = spline_cols("d")
    spl_all = {}
    for (wk, cp, s0) in (("g", inputs["L1_ffn_gate_cp"], s0_gu),
                         ("u", inputs["L1_ffn_up_cp"], s0_gu),
                         ("d", inputs["L1_ffn_down_cp"], s0_d)):
        Q, PH = spline_host_rows(np.asarray(cp, f32), s0)
        spl_all[wk] = (Q, PH)
    maps = []
    for c in range(8):
        b, r = c // 4, c % 4
        m = dict(shared)
        xs = x[b, r * TOK:(r + 1) * TOK, :]
        m["x0"] = asf(xs.T.reshape(NDC, 128, TOK))
        m["l1q_nat"] = asf(np.asarray(inputs["L1_attn_w_q"])[r * 512:(r + 1) * 512, :])
        m["l1k_nat"] = asf(np.asarray(inputs["L1_attn_w_k"])[r * 512:(r + 1) * 512, :])
        m["l1v_nat"] = asf(np.asarray(inputs["L1_attn_w_v"])[r * 512:(r + 1) * 512, :])
        m["l1o_nat"] = asf(np.asarray(inputs["L1_attn_w_o"])[:, r * 512:(r + 1) * 512])
        for wk in ("g", "u", "d"):
            Q, PH = spl_all[wk]
            csl = slice(c * CPC, (c + 1) * CPC)
            m[f"spl_{wk}_Q"] = np.ascontiguousarray(Q[:, csl])
            m[f"spl_{wk}_phi"] = np.ascontiguousarray(PH[:, csl])
        maps.append(m)
    return maps


def kernel(**inputs):
    if "nc" not in _cache:
        _cache["nc"] = build()
    nc = _cache["nc"]
    maps = prep_inputs(inputs)
    res = run_bass_kernel_spmd(nc, maps, core_ids=list(range(8)))
    out = np.zeros((2, SEQ, D_MODEL), np.float32)
    for c in range(8):
        b, r = c // 4, c % 4
        oc = res.results[c]["out"]
        out[b, r * TOK:(r + 1) * TOK, :] = oc.reshape(D_MODEL, TOK).T
    return out
